# version 16
# baseline (speedup 1.0000x reference)
import numpy as np
import ml_dtypes
import concourse.bass as bass
import concourse.mybir as mybir
from concourse.bass_core_rust import IndirectOffsetOnAxis
from concourse.bass_utils import run_bass_kernel_spmd
from contextlib import ExitStack

F32 = mybir.dt.float32
BF16 = mybir.dt.bfloat16
F8 = mybir.dt.float8e4
I32 = mybir.dt.int32
I16 = mybir.dt.int16
AF = mybir.ActivationFunctionType
ALU = mybir.AluOpType

T = 2048
D = 2048
NH, DQ, DV = 8, 128, 256
QK = 1024
I = 5632
L = 128
NCH = T // L
KB = D // 128
IB = I // 128
LNEPS = 1e-5
MHEPS = 1e-6
CAP = 15.0
QSC = np.float32(DQ) ** np.float32(-0.5)
FTILE = 512
P4B = True
P4LVL = 4
NFT = T // FTILE


class SemRef:

    def __init__(self, h, base):
        self.h = h
        self.base = base


class Sched:

    def __init__(self, nc):
        self.nc = nc
        self.items = []
        self.counts = {}

    def _abs(self, s):
        return self.counts.get(s.h, 0)

    def inc(self, sem, n=1):
        self.counts[sem.h] = self.counts.get(sem.h, 0) + n
        return self.counts[sem.h] - sem.base

    def cnt(self, sem):
        return self.counts.get(sem.h, 0) - sem.base

    def emit(self, eng_name, fbody, waits=(), incs=()):
        for s, n in incs:
            self.inc(s, n)
        wl = [(s.h, s.base + v) for s, v in waits]
        il = [(s.h, n) for s, n in incs]

        def run(eng):
            for s, v in wl:
                eng.wait_ge(s, v)
            inst = fbody(eng)
            if il:
                if inst is None:
                    inst = eng.nop()
                for s, n in il:
                    inst.then_inc(s, n)
        self.items.append((eng_name, run))

    def drain(self, eng_name):
        self.items.append((eng_name, lambda eng: eng.drain()))

    def barrier(self, sem, extra_waits=()):
        ew = dict(extra_waits) if extra_waits else {}
        tgt = self.cnt(sem) + 5
        for name in ("sync", "tensor", "vector", "scalar", "gpsimd"):
            self.emit(name, lambda e: None, waits=ew.get(name, ()), incs=[(sem, 1)])
        for name in ("sync", "tensor", "vector", "scalar", "gpsimd"):
            self.emit(name, lambda e: None, waits=[(sem, tgt)])

    def realize(self, block):
        for name in ("sync", "tensor", "vector", "scalar", "gpsimd"):
            items = [fn for e, fn in self.items if e == name]

            def body(eng, _items=items):
                for fn in _items:
                    fn(eng)
            getattr(block, name)(body)


def ceil_div(a, b):
    return (a + b - 1) // b


def build_program(debug_taps=(), phases=("p0","p1","p2","p3","pq","p4")):
    nc = bass.Bass("TRN2", target_bir_lowering=False)
    ctx = ExitStack()

    dt_in = {}

    def din(name, shape, dtype=F32):
        dt_in[name] = nc.dram_tensor(name, shape, dtype, kind="ExternalInput").ap()
        return dt_in[name]

    x_d = din("x", [T, D])
    wqT_d = din("wqT", [KB, 128, QK])
    wkT_d = din("wkT", [KB, 128, QK])
    wvT_d = din("wvT", [KB, 128, D])
    wogT_d = din("wogT", [KB, 128, D])
    woutT_d = din("woutT", [KB, 128, D])
    wigT_d = din("wigT", [KB, 128, NH])
    wfgT_d = din("wfgT", [KB, 128, NH])
    big_d = din("big", [NH, 1])
    bfg_d = din("bfg", [NH, 1])
    gamma_d = din("gammar", [1, D])
    wg_d = din("wg", [IB, 128, D])
    wu_d = din("wu", [IB, 128, D])
    wd_d = din("wd", [KB, 128, I])
    dn_d = din("dn", [I], I32)
    tri_d = din("tri", [128, 128])
    iden32_d = din("iden32", [128, 128])
    iden16_d = din("iden16", [128, 128], BF16)
    oneh_d = din("oneh", [NH, NH * 128])
    ones1_d = din("ones1", [1, 128])

    y_d = nc.dram_tensor("y", [T, D], F32, kind="ExternalOutput").ap()

    def scr(name, shape, dtype=F32):
        return nc.dram_tensor(name, shape, dtype, kind="Internal").ap()

    xnh_d = scr("xnh_s", [KB, 128, T], BF16)
    xnl_d = scr("xnl_s", [KB, 128, T], BF16)
    qxT_d = scr("qxT_s", [KB, 128, T], BF16)
    qf_d = scr("qf_s", [NH, 128, T])
    kf_d = scr("kf_s", [NH, 128, T])
    v_d = scr("v_s", [T, D])
    og_d = scr("og_s", [T, D])
    hoh_d = scr("hoh_s", [KB, 128, T], BF16)
    hol_d = scr("hol_s", [KB, 128, T], BF16)
    ya_d = scr("ya_s", [T, D])
    qwgB_d = scr("qwgB_s", [128, I // 256, KB, 256], BF16)
    qwuB_d = scr("qwuB_s", [128, I // 256, KB, 256], BF16)
    qwdB_d = scr("qwdB_s", [128, D // 256, IB, 256], BF16)
    qtT_d = scr("qtT_s", [I, FTILE], BF16)
    t_d = scr("t_s", [T, I])
    csg_d = scr("csg_s", [1, I])
    csu_d = scr("csu_s", [1, I])
    csd_d = scr("csd_s", [1, D])

    dbg = {}
    if debug_taps:
        if "p0" in debug_taps:
            dbg["xn_tm"] = scr("dbg_xn", [T, D])
            dbg["mr"] = scr("dbg_mr", [T, 1])
        if "p2" in debug_taps:
            dbg["h_sb"] = scr("dbg_h", [T, D])
            dbg["hout"] = scr("dbg_hout", [T, D])

    sb = lambda name, shape, dtype=F32: ctx.enter_context(
        nc.sbuf_tensor(name, shape, dtype))

    iden16 = sb("iden16_c", [128, 128], BF16)
    ones1 = sb("ones1_c", [1, 128])
    m_in = sb("m_in_t", [NH, 1])
    rs_x = sb("rs_x", [128, NCH])
    idxc = sb("idxc", [128, IB], I32)
    eps_ln = sb("eps_ln", [128, 1])
    eps_mh = sb("eps_mh", [128, 1])

    sch = Sched(nc)

    sem_scope = {}
    sem_free = {False: [], "hw": [], "sw": []}
    sem_kind = {}
    sem_n = [0]

    def S(name, dma=False):
        if name not in sem_scope:
            if sem_free[dma]:
                h = sem_free[dma].pop()
            else:
                h = ctx.enter_context(nc.semaphore(f"sem{sem_n[0]}"))
                sem_n[0] += 1
            sem_scope[name] = SemRef(h, sch.counts.get(h, 0))
            sem_kind[name] = dma
        return sem_scope[name]

    def new_scope():
        for n in list(sem_scope):
            if n != "bar" and not n.startswith("dbg"):
                sem_free[sem_kind[n]].append(sem_scope.pop(n).h)

    def emit_const_loads():
        s_cl = S("const_ld", dma="hw")
        for dst, src in ((iden16, iden16_d), (ones1, ones1_d)):
            sch.emit("sync", lambda e, d=dst, s=src: e.dma_start(out=d[:], in_=s[:, :]),
                     incs=[(s_cl, 16)])
        bigc = sb("bigc", [NH, 1])
        bfgc = sb("bfgc", [NH, 1])
        sch.emit("sync", lambda e: e.dma_start(out=bigc[:], in_=big_d[:, :]),
                 incs=[(s_cl, 16)])
        sch.emit("sync", lambda e: e.dma_start(out=bfgc[:], in_=bfg_d[:, :]),
                 incs=[(s_cl, 16)])
        dnb = sb("dnb", [IB, 128], I32)
        sch.emit("sync", lambda e: e.dma_start(
            out=dnb[:], in_=dn_d.rearrange("(b p) -> b p", p=128)),
            incs=[(s_cl, 16)])
        n_dma = sch.cnt(s_cl)
        sch.emit("vector", lambda e: e.memset(eps_ln[:], float(LNEPS)))
        sch.emit("vector", lambda e: e.memset(eps_mh[:], float(MHEPS)))
        sch.emit("vector", lambda e: e.tensor_scalar_mul(bigc[:], bigc[:], 1.0 / CAP),
                 waits=[(s_cl, n_dma)])
        sch.emit("vector", lambda e: e.tensor_scalar_mul(bfgc[:], bfgc[:], 1.0 / CAP))
        sch.drain("vector")
        return bigc, bfgc, dnb

    bigc, bfgc, dnb = emit_const_loads()
    midctx = ExitStack()
    ip_t = midctx.enter_context(nc.sbuf_tensor("ip_t", [NH, T], F32))
    fp_t = midctx.enter_context(nc.sbuf_tensor("fp_t", [NH, T], F32))

    def phase0():
        pctx = ExitStack()
        sb = lambda name, shape, dtype=F32: pctx.enter_context(
            nc.sbuf_tensor(name, shape, dtype))
        s_x = [S(f"p0_x{i}", dma="hw") for i in range(2)]
        s_ev_v = S("p0_ev_v")
        s_ev_s = S("p0_ev_s")
        s_st = [S(f"p0_st{i}", dma="sw") for i in range(2)]
        s_tile = S("p0_tile")
        s_tp = S("p0_tp")
        s_h = S("p0_h")
        s_n = S("p0_n")
        s_v = S("p0_v")
        xt = [sb(f"p0_xt{i}", [128, D]) for i in range(2)]
        xn = sb("p0_xn", [128, D])
        xh = [sb(f"p0_xh{i}", [128, D], BF16) for i in range(2)]
        xl = [sb(f"p0_xl{i}", [128, D], BF16) for i in range(2)]
        qxb = [sb(f"p0_qxb{i}", [128, D], BF16) for i in range(2)]
        qx8 = sb("p0_qx8", [128, D], F8)
        mean = sb("p0_mean", [128, 1])
        var = sb("p0_var", [128, 1])
        rstd = sb("p0_rstd", [128, 1])
        sqs = sb("p0_sq", [128, D])
        mr = sb("p0_mr", [128, 1])
        qsc = sb("p0_qsc", [128, 1])
        stg = [sb(f"p0_stg{i}", [128, KB, 128], BF16) for i in range(2)]
        ps = [pctx.enter_context(nc.psum_tensor(f"p0_ps{i}", [128, 128], BF16))
              for i in range(4)]

        for it in range(NCH):
            par = it % 2
            w = [(s_tile, it - 1)] if it >= 2 else []
            sch.emit("sync", lambda e, it=it, par=par: e.dma_start(
                out=xt[par][:], in_=x_d[it * 128:(it + 1) * 128, :]),
                waits=w, incs=[(s_x[par], 16)])
            wld = [(s_x[par], 16 * (it // 2 + 1))]
            wtr = [(s_tp, 48 * (it - 1))] if it >= 2 else []
            sch.emit("vector", lambda e, par=par: e.tensor_reduce(
                mean[:], xt[par][:], mybir.AxisListType.X, ALU.add),
                waits=wld + wtr)
            sch.drain("vector")
            sch.emit("vector", lambda e: e.tensor_scalar_mul(mean[:], mean[:], 1.0 / D))
            sch.drain("vector")
            wdbg = ([(S("dbg_st", dma="sw"), sch.cnt(S("dbg_st", dma="sw")))]
                    if "p0" in debug_taps else [])
            sch.emit("vector", lambda e, par=par: e.tensor_scalar(
                xn[:], xt[par][:], mean[:, 0:1], None, ALU.subtract),
                waits=wdbg, incs=[(S("p0_c"), 1)])
            sch.emit("scalar", lambda e: e.activation(
                sqs[:], xn[:], AF.Square, accum_out=var[:]),
                waits=[(S("p0_c"), sch.cnt(S("p0_c")))])
            sch.drain("scalar")
            sch.emit("scalar", lambda e: e.activation(
                rstd[:], var[:], AF.Sqrt, bias=eps_ln[:, 0:1], scale=1.0 / D),
                incs=[(s_v, 1)])
            sch.emit("vector", lambda e: e.reciprocal(rstd[:], rstd[:]),
                     waits=[(s_v, sch.cnt(s_v))])
            sch.drain("vector")
            sch.emit("vector", lambda e: e.tensor_scalar(
                xn[:], xn[:], rstd[:, 0:1], None, ALU.mult), incs=[(s_n, 1)])
            sch.drain("vector")
            if "p0" in debug_taps:
                sch.emit("gpsimd", lambda e, it=it: e.dma_start(
                    out=dbg["xn_tm"][it * 128:(it + 1) * 128, :], in_=xn[:]),
                    waits=[(s_n, sch.cnt(s_n))], incs=[(S("dbg_st", dma="sw"), 16)])
            sch.emit("vector", lambda e: e.tensor_reduce(
                mr[:], xn[:], mybir.AxisListType.X, ALU.max,
                apply_absolute_value=True))
            sch.drain("vector")
            sch.emit("vector", lambda e: e.reciprocal(qsc[:], mr[:]))
            sch.emit("vector", lambda e, it=it: e.tensor_scalar(
                rs_x[:, it:it + 1], mr[:], 1.0 / 224.0, None, ALU.mult))
            sch.drain("vector")
            sch.emit("vector", lambda e: e.tensor_scalar(
                qx8[:], xn[:], qsc[:, 0:1], 224.0, ALU.mult, ALU.mult))
            sch.drain("vector")
            sch.emit("vector", lambda e, par=par: e.tensor_copy(qxb[par][:], qx8[:]))
            if "p0" in debug_taps:
                sch.emit("gpsimd", lambda e, it=it: e.dma_start(
                    out=dbg["mr"][it * 128:(it + 1) * 128, :], in_=mr[:]),
                    incs=[(S("dbg_st", dma="sw"), 16)])
            sch.emit("scalar", lambda e, par=par: e.activation(
                xh[par][:], xn[:], AF.Copy),
                waits=[(s_n, sch.cnt(s_n))], incs=[(s_h, 1)])
            sch.emit("vector", lambda e, par=par: e.tensor_tensor(
                xl[par][:], xn[:], xh[par][:], ALU.subtract),
                waits=[(s_h, sch.cnt(s_h))], incs=[(s_tile, 1)])
            sch.drain("vector")
            nt = sch.cnt(s_tile)
            if it == 0:
                bank_last = {}
            for si, (src_t, dst_d) in enumerate(
                    ((qxb[par], qxT_d), (xh[par], xnh_d), (xl[par], xnl_d))):
                spar = (it * 3 + si) % 2
                nprev = (it * 3 + si) - 2
                wstg = [(s_st[spar], 16 * (nprev // 2 + 1))] if nprev >= 0 else []
                for fb in range(KB):
                    pb = (it * 48 + si * KB + fb) % 4
                    wtp = [bank_last[pb]] if pb in bank_last else []
                    sch.emit("tensor", lambda e, src_t=src_t, fb=fb, pb=pb: e.transpose(
                        ps[pb][:], src_t[:, fb * 128:(fb + 1) * 128], iden16[:]),
                        waits=[(s_tile, nt)] + wtp, incs=[(s_tp, 1)])
                    eng = "vector" if fb % 2 == 0 else "scalar"
                    sev = s_ev_v if fb % 2 == 0 else s_ev_s
                    if fb % 2 == 0:
                        fevac = (lambda e, fb=fb, pb=pb, spar=spar:
                                 e.tensor_copy(stg[spar][:, fb, :], ps[pb][:]))
                    else:
                        fevac = (lambda e, fb=fb, pb=pb, spar=spar:
                                 e.activation(stg[spar][:, fb, :], ps[pb][:], AF.Copy))
                    sch.emit(eng, fevac,
                             waits=[(s_tp, sch.cnt(s_tp))] + wstg,
                             incs=[(sev, 1)])
                    bank_last[pb] = (sev, sch.cnt(sev))
                wev = [(s_ev_v, sch.cnt(s_ev_v)), (s_ev_s, sch.cnt(s_ev_s))]
                sch.emit("gpsimd", lambda e, dst_d=dst_d, it=it, spar=spar: e.dma_start(
                    out=dst_d[:, :, it * 128:(it + 1) * 128].rearrange("n p t -> p n t"),
                    in_=stg[spar][:]),
                    waits=wev, incs=[(s_st[spar], 16)])
        ntot = NCH * 3
        sch.barrier(S("bar"), extra_waits={
            "sync": [(s_st[0], 16 * ((ntot + 1) // 2)),
                     (s_st[1], 16 * (ntot // 2))]})
        new_scope()
        pctx.close()

    if "p0" in phases:
        phase0()

    def load_split_weight(tag, w_d, OF, wh, wl, s_done, pre_wait=()):
        pctx = ExitStack()
        wf = [pctx.enter_context(nc.sbuf_tensor(f"{tag}_wf{i}", [128, OF], F32))
              for i in range(2)]
        s_ld = [S(f"{tag}_wld{i}", dma="hw") for i in range(2)]
        s_sp = S(f"{tag}_wsp")
        for k in range(KB):
            par = k % 2
            w = [(s_sp, k - 1)] if k >= 2 else list(pre_wait)
            sch.emit("sync", lambda e, k=k, par=par: e.dma_start(
                out=wf[par][:], in_=w_d[k, :, :]), waits=w, incs=[(s_ld[par], 16)])
            sch.emit("scalar", lambda e, k=k, par=par: e.activation(
                wh[:, k, :], wf[par][:], AF.Copy),
                waits=[(s_ld[par], 16 * (k // 2 + 1))], incs=[(S(f"{tag}_wh"), 1)])
            sch.emit("vector", lambda e, k=k, par=par: e.tensor_tensor(
                wl[:, k, :], wf[par][:], wh[:, k, :], ALU.subtract),
                waits=[(S(f"{tag}_wh"), sch.cnt(S(f"{tag}_wh")))],
                incs=[(s_sp, 1)])
        sch.emit("tensor", lambda e: None, waits=[(s_sp, KB)], incs=[(s_done, 1)])
        pctx.close()
        return s_sp

    def phase1_qk():
        pctx = ExitStack()
        sb = lambda name, shape, dtype=F32: pctx.enter_context(
            nc.sbuf_tensor(name, shape, dtype))
        wqh = sb("p1_wqh", [128, KB, QK], BF16)
        wql = sb("p1_wql", [128, KB, QK], BF16)
        wkh = sb("p1_wkh", [128, KB, QK], BF16)
        wkl = sb("p1_wkl", [128, KB, QK], BF16)
        wgh = sb("p1_wgh", [128, KB, 2 * NH], BF16)
        wgl = sb("p1_wgl", [128, KB, 2 * NH], BF16)
        s_wrdy = S("p1qk_wrdy")
        spq = load_split_weight("p1wq", wqT_d, QK, wqh, wql, s_wrdy)
        spk = load_split_weight("p1wk", wkT_d, QK, wkh, wkl, s_wrdy,
                                pre_wait=[(spq, KB)])
        s_gld = [S(f"p1_gld{i}", dma="hw") for i in range(2)]
        gfk = [sb(f"p1_gfk{i}", [128, 2 * NH], F32) for i in range(2)]
        for k in range(KB):
            par = k % 2
            w = [(S("p1_gsp"), k - 1)] if k >= 2 else [(spk, KB)]
            sch.emit("sync", lambda e, k=k, par=par: e.dma_start(
                out=gfk[par][:, 0:NH], in_=wigT_d[k, :, :]), waits=w,
                incs=[(s_gld[par], 16)])
            sch.emit("sync", lambda e, k=k, par=par: e.dma_start(
                out=gfk[par][:, NH:2 * NH], in_=wfgT_d[k, :, :]),
                incs=[(s_gld[par], 16)])
            sch.emit("scalar", lambda e, k=k, par=par: e.activation(
                wgh[:, k, :], gfk[par][:], AF.Copy),
                waits=[(s_gld[par], 32 * (k // 2 + 1))], incs=[(S("p1_gh"), 1)])
            sch.emit("vector", lambda e, k=k, par=par: e.tensor_tensor(
                wgl[:, k, :], gfk[par][:], wgh[:, k, :], ALU.subtract),
                waits=[(S("p1_gh"), sch.cnt(S("p1_gh")))], incs=[(S("p1_gsp"), 1)])
        sch.emit("tensor", lambda e: None, waits=[(S("p1_gsp"), KB)],
                 incs=[(s_wrdy, 1)])
        TCH = 256
        NTC = T // TCH
        xh = [sb(f"p1qk_xh{i}", [128, KB, TCH], BF16) for i in range(2)]
        xl = [sb(f"p1qk_xl{i}", [128, KB, TCH], BF16) for i in range(2)]
        s_x = [S(f"p1qk_x{i}", dma="hw") for i in range(2)]
        s_mm = S("p1qk_mm")
        s_ev_v = S("p1qk_ev_v")
        s_ev_s = S("p1qk_ev_s")
        s_st = [S(f"p1qk_st{i}", dma="sw") for i in range(2)]
        qo = [sb(f"p1qk_qo{i}", [128, TCH], F32) for i in range(2)]
        NPB = 6
        psb = [pctx.enter_context(nc.psum_tensor(f"p1qk_ps{i}", [128, TCH], F32))
               for i in range(NPB)]
        ps = [t[:] for t in psb]
        psg = pctx.enter_context(nc.psum_tensor("p1qk_psg", [NH, TCH], F32))
        psg2 = pctx.enter_context(nc.psum_tensor("p1qk_psg2", [NH, TCH], F32))
        bank_last = {}
        ng = 0
        for tc in range(NTC):
            par = tc % 2
            w = ([(s_mm, (tc - 1) * 16), (S("p1_gmm"), 2 * (tc - 1))]
                 if tc >= 2 else [(s_wrdy, 3)])
            sch.emit("sync", lambda e, tc=tc, par=par: e.dma_start(
                out=xh[par][:],
                in_=xnh_d[:, :, tc * TCH:(tc + 1) * TCH].rearrange("n p t -> p n t")),
                waits=w, incs=[(s_x[par], 16)])
            sch.emit("sync", lambda e, tc=tc, par=par: e.dma_start(
                out=xl[par][:],
                in_=xnl_d[:, :, tc * TCH:(tc + 1) * TCH].rearrange("n p t -> p n t")),
                incs=[(s_x[par], 16)])
            wx = [(s_x[par], 32 * (tc // 2 + 1)), (s_wrdy, 3)]
            for ob in range(16):
                wres_h = (wqh if ob < 8 else wkh)
                wres_l = (wql if ob < 8 else wkl)
                qb = ob % 8
                pb = ng % NPB
                wpb = [bank_last[pb]] if pb in bank_last else []
                last = None
                for k in range(KB):
                    for pi, (wv_, xv_) in enumerate(
                            ((wres_h, xh[par]), (wres_l, xh[par]), (wres_h, xl[par]))):
                        def mmf(e, wv_=wv_, xv_=xv_, k=k, qb=qb, pb=pb,
                                first=(k == 0 and pi == 0),
                                stop=(k == KB - 1 and pi == 2), par=par):
                            return e.matmul(
                                ps[pb], wv_[:, k, qb * 128:(qb + 1) * 128],
                                xv_[:, k, :], start=first, stop=stop)
                        if k == 0 and pi == 0:
                            sch.emit("tensor", mmf, waits=wx + wpb,
                                     incs=[(s_mm, 1)] if (k == KB - 1) else [])
                        elif k == KB - 1 and pi == 2:
                            sch.emit("tensor", mmf, incs=[(s_mm, 1)])
                        else:
                            sch.emit("tensor", mmf)
                ng += 1
                eng = "vector" if ob % 2 == 0 else "scalar"
                sev = s_ev_v if ob % 2 == 0 else s_ev_s
                spar = ng % 2
                nst_prev = ng - 3
                wst = [(s_st[spar], 16 * ((ng - 2 - 1) // 2 + 1))] if ng > 2 else []
                if eng == "vector":
                    fev = lambda e, pb=pb, spar=spar: e.tensor_copy(qo[spar][:], ps[pb])
                else:
                    fev = lambda e, pb=pb, spar=spar: e.activation(qo[spar][:], ps[pb], AF.Copy)
                sch.emit(eng, fev, waits=[(s_mm, ng)] + wst, incs=[(sev, 1)])
                bank_last[pb] = (sev, sch.cnt(sev))
                dst = qf_d if ob < 8 else kf_d
                sch.emit("gpsimd", lambda e, dst=dst, qb=qb, tc=tc, spar=spar:
                         e.dma_start(out=dst[qb, :, tc * TCH:(tc + 1) * TCH],
                                     in_=qo[spar][:]),
                         waits=[(sev, sch.cnt(sev))], incs=[(s_st[spar], 16)])
            for gi, pstgt in ((0, psg), (1, psg2)):
                for k in range(KB):
                    for pi, (wv_, xv_) in enumerate(
                            ((wgh, xh[par]), (wgl, xh[par]), (wgh, xl[par]))):
                        def gmf(e, wv_=wv_, xv_=xv_, k=k, par=par, gi=gi,
                                pstgt=pstgt,
                                first=(k == 0 and pi == 0),
                                stop=(k == KB - 1 and pi == 2)):
                            return e.matmul(
                                pstgt[:], wv_[:, k, gi * NH:(gi + 1) * NH],
                                xv_[:, k, :], start=first, stop=stop)
                        if k == 0 and pi == 0:
                            wg_ = wx + ([bank_last["g", gi]]
                                        if ("g", gi) in bank_last else [])
                            sch.emit("tensor", gmf, waits=wg_)
                        elif k == KB - 1 and pi == 2:
                            sch.emit("tensor", gmf, incs=[(S("p1_gmm"), 1)])
                        else:
                            sch.emit("tensor", gmf)
            sch.emit("scalar", lambda e, tc=tc: e.activation(
                ip_t[:, tc * TCH:(tc + 1) * TCH], psg[0:NH, :], AF.Tanh,
                bias=bigc[:, 0:1], scale=1.0 / CAP),
                waits=[(S("p1_gmm"), sch.cnt(S("p1_gmm")))],
                incs=[(S("p1_gev"), 1)])
            bank_last["g", 0] = (S("p1_gev"), sch.cnt(S("p1_gev")))
            sch.emit("scalar", lambda e, tc=tc: e.activation(
                fp_t[:, tc * TCH:(tc + 1) * TCH], psg2[0:NH, :], AF.Tanh,
                bias=bfgc[:, 0:1], scale=1.0 / CAP), incs=[(S("p1_gev"), 1)])
            bank_last["g", 1] = (S("p1_gev"), sch.cnt(S("p1_gev")))
            sch.drain("scalar")
            sch.emit("scalar", lambda e, tc=tc: e.activation(
                ip_t[:, tc * TCH:(tc + 1) * TCH], ip_t[:, tc * TCH:(tc + 1) * TCH],
                AF.Copy, scale=float(CAP)))
            sch.drain("scalar")
            sch.emit("scalar", lambda e, tc=tc: e.activation(
                fp_t[:, tc * TCH:(tc + 1) * TCH], fp_t[:, tc * TCH:(tc + 1) * TCH],
                AF.Copy, scale=float(CAP)), incs=[(S("p1_gfin"), 1)])
        nq = NTC * 16
        sch.barrier(S("bar"), extra_waits={
            "sync": [(s_st[0], 16 * (nq // 2)), (s_st[1], 16 * (nq // 2))],
            "scalar": [(S("p1_gfin"), sch.cnt(S("p1_gfin")))]})
        new_scope()
        pctx.close()

    def phase1_tm(tag, w_d, out_d, src_h_d=None, src_l_d=None):
        pctx = ExitStack()
        sb = lambda name, shape, dtype=F32: pctx.enter_context(
            nc.sbuf_tensor(name, shape, dtype))
        if src_h_d is None:
            src_h_d, src_l_d = xnh_d, xnl_d
        wh_ = sb(f"{tag}_wh", [128, KB, D], BF16)
        wl_ = sb(f"{tag}_wl", [128, KB, D], BF16)
        s_wrdy = S(f"{tag}_wrdy")
        load_split_weight(f"{tag}w", w_d, D, wh_, wl_, s_wrdy)
        xh = [sb(f"{tag}_xh{i}", [128, KB, 128], BF16) for i in range(2)]
        xl = [sb(f"{tag}_xl{i}", [128, KB, 128], BF16) for i in range(2)]
        s_x = [S(f"{tag}_x{i}", dma="hw") for i in range(2)]
        s_mm = S(f"{tag}_mm")
        s_ev_v = S(f"{tag}_ev_v")
        s_ev_s = S(f"{tag}_ev_s")
        s_st = [S(f"{tag}_st{i}", dma="sw") for i in range(2)]
        vt = [sb(f"{tag}_vt{i}", [128, D], F32) for i in range(2)]
        ps = [pctx.enter_context(nc.psum_tensor(f"{tag}_ps{i}", [128, 512], F32))
              for i in range(8)]
        bank_last = {}
        ng = 0
        for tt in range(NCH):
            par = tt % 2
            w = [(s_mm, (tt - 1) * 4)] if tt >= 2 else [(s_wrdy, 1)]
            sch.emit("sync", lambda e, tt=tt, par=par: e.dma_start(
                out=xh[par][:],
                in_=src_h_d[:, :, tt * 128:(tt + 1) * 128].rearrange("n p t -> p n t")),
                waits=w, incs=[(s_x[par], 16)])
            sch.emit("sync", lambda e, tt=tt, par=par: e.dma_start(
                out=xl[par][:],
                in_=src_l_d[:, :, tt * 128:(tt + 1) * 128].rearrange("n p t -> p n t")),
                incs=[(s_x[par], 16)])
            wx = [(s_x[par], 32 * (tt // 2 + 1)), (s_wrdy, 1)]
            for of in range(4):
                pb = ng % 8
                wpb = [bank_last[pb]] if pb in bank_last else []
                for k in range(KB):
                    for pi, (xv_, wv_) in enumerate(
                            ((xh[par], wh_), (xh[par], wl_), (xl[par], wh_))):
                        def mmf(e, xv_=xv_, wv_=wv_, k=k, of=of, pb=pb,
                                first=(k == 0 and pi == 0),
                                stop=(k == KB - 1 and pi == 2)):
                            return e.matmul(
                                ps[pb][:], xv_[:, k, :],
                                wv_[:, k, of * 512:(of + 1) * 512],
                                start=first, stop=stop)
                        if k == 0 and pi == 0:
                            sch.emit("tensor", mmf, waits=wx + wpb)
                        elif k == KB - 1 and pi == 2:
                            sch.emit("tensor", mmf, incs=[(s_mm, 1)])
                        else:
                            sch.emit("tensor", mmf)
                ng += 1
                eng = "vector" if of % 2 == 0 else "scalar"
                sev = s_ev_v if of % 2 == 0 else s_ev_s
                wst = ([(s_st[par], 16 * ((tt - 2) // 2 + 1))]
                       if tt >= 2 and of in (0, 1) else [])
                if eng == "vector":
                    fev = (lambda e, pb=pb, of=of, par=par:
                           e.tensor_copy(vt[par][:, of * 512:(of + 1) * 512], ps[pb][:]))
                else:
                    fev = (lambda e, pb=pb, of=of, par=par:
                           e.activation(vt[par][:, of * 512:(of + 1) * 512], ps[pb][:], AF.Copy))
                sch.emit(eng, fev, waits=[(s_mm, ng)] + wst, incs=[(sev, 1)])
                bank_last[pb] = (sev, sch.cnt(sev))
            wev = [(s_ev_v, sch.cnt(s_ev_v)), (s_ev_s, sch.cnt(s_ev_s))]
            sch.emit("gpsimd", lambda e, tt=tt, par=par: e.dma_start(
                out=out_d[tt * 128:(tt + 1) * 128, :], in_=vt[par][:]),
                waits=wev, incs=[(s_st[par], 16)])
        sch.barrier(S("bar"), extra_waits={
            "sync": [(s_st[0], 16 * (sch.cnt(s_st[0]) // 16)),
                     (s_st[1], 16 * (sch.cnt(s_st[1]) // 16))]})
        new_scope()
        pctx.close()

    if "p1" in phases:
        phase1_qk()
        phase1_tm("p1v", wvT_d, v_d)
        phase1_tm("p1og", wogT_d, og_d)

    def phase2():
        pctx = ExitStack()
        sb = lambda name, shape, dtype=F32: pctx.enter_context(
            nc.sbuf_tensor(name, shape, dtype))
        psum = lambda name, shape: pctx.enter_context(
            nc.psum_tensor(name, shape, F32))
        tri = sb("p2_tri", [128, 128])
        iden32 = sb("p2_iden32", [128, 128])
        oneh = sb("p2_oneh", [NH, NH * 128])
        gamma_bc = sb("p2_gbc", [128, D])
        gr = sb("p2_gr", [1, D])
        s_cl = S("p2_cl", dma="hw")
        for dst, srcd in ((tri, tri_d), (iden32, iden32_d), (oneh, oneh_d)):
            sch.emit("sync", lambda e, d=dst, s=srcd: e.dma_start(
                out=d[:], in_=s[:, :]), incs=[(s_cl, 16)])
        sch.emit("sync", lambda e: e.dma_start(out=gr[:], in_=gamma_d[:, :]),
                 incs=[(s_cl, 16)])
        ncl = sch.cnt(s_cl)
        pbc_ctx = ExitStack()
        pbc = pbc_ctx.enter_context(nc.psum_tensor("p2_pbc", [128, 512], F32))
        s_gbc = S("p2_gbc_mm")
        s_gbe = S("p2_gbc_ev")
        for c4 in range(4):
            wv_ = [(s_cl, ncl)] if c4 == 0 else [(s_gbe, sch.cnt(s_gbe))]
            sch.emit("tensor", lambda e, c4=c4: e.matmul(
                pbc[:], ones1[:], gr[0:1, c4 * 512:(c4 + 1) * 512],
                start=True, stop=True), waits=wv_, incs=[(s_gbc, 1)])
            sch.emit("vector", lambda e, c4=c4: e.tensor_copy(
                gamma_bc[:, c4 * 512:(c4 + 1) * 512], pbc[:]),
                waits=[(s_gbc, sch.cnt(s_gbc))], incs=[(s_gbe, 1)])
        st = [sb(f"p2_st{i}", [128, NH, DV + 1]) for i in range(2)]
        s_init = S("p2_init")
        sch.emit("vector", lambda e: e.memset(st[0][:], 0.0))
        sch.emit("vector", lambda e: e.memset(m_in[:], 0.0), incs=[(s_init, 1)])
        qs = [sb(f"p2_q{i}", [128, NH, 128]) for i in range(2)]
        ks = [sb(f"p2_k{i}", [128, NH, 128]) for i in range(2)]
        vb = [sb(f"p2_v{i}", [128, NH, DV + 1]) for i in range(2)]
        ogs = [sb(f"p2_og{i}", [128, D]) for i in range(2)]
        for i in range(2):
            sch.emit("vector", lambda e, i=i: e.memset(vb[i][:, :, DV:DV + 1], 1.0),
                     incs=[(s_init, 1)])
        ga = [sb(f"p2_ga{i}", [NH, 128]) for i in range(2)]
        gb = [sb(f"p2_gb{i}", [NH, 128]) for i in range(2)]
        gem = [sb(f"p2_gem{i}", [NH, 128]) for i in range(2)]
        gupd = [sb(f"p2_gupd{i}", [NH, 128]) for i in range(2)]
        gps = [sb(f"p2_gps{i}", [NH, 128]) for i in range(2)]
        gqT = [sb(f"p2_gqT{i}", [128, 4 * NH]) for i in range(2)]
        gtmp = sb("p2_gtmp", [NH + 1, 128])
        nal = sb("p2_nal", [NH, 1])
        dec8 = sb("p2_dec8", [NH, 1])
        decrow = sb("p2_decrow", [1, NH])
        mrow = sb("p2_mrow", [1, NH])
        decbc = [sb(f"p2_decbc{i}", [128, NH]) for i in range(2)]
        zer8 = sb("p2_zer8", [NH, 128])
        sch.emit("vector", lambda e: e.memset(zer8[:], 0.0), incs=[(s_init, 1)])
        dm0 = [sb(f"p2_dm0{i}", [128, 128]) for i in range(2)]
        dms = [sb(f"p2_dms{i}", [128, 128]) for i in range(2)]
        ssb = [sb(f"p2_ss{i}", [128, 128]) for i in range(2)]
        hts = [sb(f"p2_ht{i}", [128, DV + 1]) for i in range(2)]
        kts = [sb(f"p2_kt{i}", [128, 128]) for i in range(2)]
        hn = [sb(f"p2_hn{i}", [128, NH, DV + 1]) for i in range(2)]
        qn8 = sb("p2_qn8", [128, NH])
        den8 = sb("p2_den8", [128, NH])
        rec8 = sb("p2_rec8", [128, NH])
        hsb = sb("p2_h", [128, D])
        mean8 = sb("p2_mean8", [128, NH])
        sq = sb("p2_sq", [128, D])
        ssum8 = sb("p2_ssum8", [128, NH])
        rstd8 = sb("p2_rstd8", [128, NH])
        houtT = sb("p2_houtT", [128, KB, 128])
        hh = sb("p2_hh", [128, KB, 128], BF16)
        hl = sb("p2_hl", [128, KB, 128], BF16)
        pbc_ctx.close()
        pb = [psum(f"p2_pb{i}", [128, 512]) for i in range(8)]
        s_ld = [S(f"p2_ld{i}", dma="hw") for i in range(2)]
        s_pe = S("p2_pe")
        s_v = S("p2_v")
        s_a = S("p2_a")
        s_g = S("p2_g")
        s_stst = [S(f"p2_stst{i}", dma="sw") for i in range(2)]
        G = {}
        G["bank", 0] = (s_gbe, 4)

        def pe(f, extra=(), key=None):
            sch.emit("tensor", f, waits=list(extra) + ([G[key]] if key in G else []),
                     incs=[(s_pe, 1)])
            return (s_pe, sch.cnt(s_pe))

        def ve(f, extra=()):
            sch.emit("vector", f, waits=list(extra), incs=[(s_v, 1)])
            return (s_v, sch.cnt(s_v))

        def ae(f, extra=()):
            sch.emit("scalar", f, waits=list(extra), incs=[(s_a, 1)])
            return (s_a, sch.cnt(s_a))

        def ge(f, extra=()):
            sch.emit("gpsimd", f, waits=list(extra), incs=[(s_g, 1)])
            return (s_g, sch.cnt(s_g))

        def emit_prologue(c):
            par = c % 2
            c0 = c * 128
            wld = ([G["ld", par], G["ldU", par]] if ("ld", par) in G
                   else [(s_init, sch.cnt(s_init))])
            sch.emit("sync", lambda e, c0=c0, par=par: e.dma_start(
                out=qs[par][:], in_=qf_d[:, :, c0:c0 + 128].rearrange("h p t -> p h t")),
                waits=wld, incs=[(s_ld[par], 16)])
            sch.emit("sync", lambda e, c0=c0, par=par: e.dma_start(
                out=ks[par][:], in_=kf_d[:, :, c0:c0 + 128].rearrange("h p t -> p h t")),
                incs=[(s_ld[par], 16)])
            sch.emit("sync", lambda e, c0=c0, par=par: e.dma_start(
                out=vb[par][:, :, 0:DV],
                in_=v_d[c0:c0 + 128, :].rearrange("p (h v) -> p h v", h=NH)),
                incs=[(s_ld[par], 16)])
            sch.emit("sync", lambda e, c0=c0, par=par: e.dma_start(
                out=ogs[par][:], in_=og_d[c0:c0 + 128, :]),
                incs=[(s_ld[par], 16)])
            nld = 64 * (c // 2 + 1)
            wch = [(s_ld[par], nld)]
            ipc = ip_t[:, c0:c0 + 128]
            fpc = fp_t[:, c0:c0 + 128]
            wgq = [G["gq", par]] if ("gq", par) in G else []
            wgt = [G["gtmp"]] if "gtmp" in G else []
            a1 = ae(lambda e, fpc=fpc: e.activation(gtmp[0:NH, :], fpc, AF.Exp,
                                                    scale=-1.0), extra=wgt)
            v1 = ve(lambda e: e.tensor_scalar_add(gtmp[0:NH, :], gtmp[0:NH, :], 1.0), extra=[a1])
            a2 = ae(lambda e: e.activation(gtmp[0:NH, :], gtmp[0:NH, :], AF.Ln), extra=[v1])
            v2 = ve(lambda e, par=par: e.tensor_tensor_scan(
                gps[par][:], gtmp[0:NH, :], zer8[:], 0.0,
                ALU.add, ALU.add), extra=[a2] + wgq)
            v3 = ve(lambda e, par=par, ipc=ipc: e.tensor_tensor(
                gb[par][:], ipc, gps[par][:], ALU.add), extra=[v2])
            sch.drain("vector")
            wmin = [G["m_in"]] if "m_in" in G else []
            v4 = ve(lambda e, par=par: e.tensor_tensor_scan(
                ga[par][:], zer8[:], gb[par][:], m_in[:, 0:1],
                ALU.add, ALU.max), extra=wmin)
            sch.drain("vector")
            v5 = ve(lambda e, par=par: e.tensor_scalar(
                nal[:], ga[par][:, 127:128], -1.0, None, ALU.mult),
                extra=[G["nal"]] if "nal" in G else [])
            a3 = ae(lambda e, par=par: e.activation(
                dec8[:], ga[par][:, 127:128], AF.Exp, bias=m_in[:, 0:1],
                scale=-1.0), extra=[v4] + ([G["dec8"]] if "dec8" in G else []))
            v6 = ve(lambda e, par=par: e.tensor_tensor(
                gtmp[0:NH, 0:1], ga[par][:, 127:128],
                gps[par][:, 127:128], ALU.subtract), extra=[v4])
            sch.drain("vector")
            v7 = ve(lambda e, par=par: e.tensor_tensor(
                gem[par][:], gps[par][:], ga[par][:], ALU.subtract))
            a4 = ae(lambda e, par=par: e.activation(
                gem[par][:], gem[par][:], AF.Exp), extra=[v7])
            a5 = ae(lambda e, par=par: e.activation(
                gupd[par][:], gb[par][:], AF.Exp, bias=nal[:, 0:1]), extra=[v5])
            v9 = a3
            v9b = ve(lambda e: e.tensor_copy(gtmp[0:NH, 2:3], m_in[:]))
            sch.drain("vector")
            v8 = ve(lambda e: e.tensor_copy(m_in[:], gtmp[0:NH, 0:1]),
                    extra=[a3, v6, v9b])
            G["m_in"] = v8
            G["nal"] = a5
            wgqT = [G["gqT", par]] if ("gqT", par) in G else []
            e1 = None
            for gi, (gt_, gw_) in enumerate(
                    ((ga[par], v4), (gb[par], v3), (gem[par], a4), (gupd[par], a5))):
                wb0 = [G["bank", 0]] if ("bank", 0) in G else []
                tgi = pe(lambda e, gt_=gt_, gi=gi: e.transpose(
                    pb[0][:, gi * NH:(gi + 1) * NH], gt_[:],
                    iden32[0:NH, 0:NH]), extra=[gw_] + wb0)
                e1 = ve(lambda e, par=par, gi=gi: e.tensor_copy(
                    gqT[par][:, gi * NH:(gi + 1) * NH],
                    pb[0][:, gi * NH:(gi + 1) * NH]), extra=[tgi] + wgqT)
                G["bank", 0] = e1
            wb1 = [G["bank", 1]] if ("bank", 1) in G else []
            t2 = pe(lambda e: e.transpose(
                pb[1][0:1, 0:NH], dec8[:, 0:1], iden32[0:NH, 0:NH]),
                extra=[v9, wmin0] + wb1 if False else [v9] + wb1)
            wdr = [G["decrow"]] if "decrow" in G else []
            e2 = ve(lambda e: e.tensor_copy(decrow[:], pb[1][0:1, 0:NH]),
                    extra=[t2] + wdr)
            G["dec8"] = t2
            t2b = pe(lambda e: e.transpose(
                pb[1][0:1, 0:NH], gtmp[0:NH, 2:3], iden32[0:NH, 0:NH]),
                extra=[v9b, e2])
            wg2 = [G["gtmpN"]] if "gtmpN" in G else []
            e2b = ve(lambda e: e.tensor_copy(mrow[0:1, :], pb[1][0:1, 0:NH]),
                     extra=[t2b] + wg2)
            G["bank", 1] = e2b
            sch.drain("vector")
            wdb = [G["decbc", par]] if ("decbc", par) in G else []
            t3 = pe(lambda e: e.matmul(pb[1][:, 0:NH], ones1[:], decrow[0:1, :],
                                       start=True, stop=True),
                    extra=[e2] + wdb, key=("bank", 1))
            e3 = ve(lambda e, par=par: e.tensor_copy(decbc[par][:], pb[1][:, 0:NH]),
                    extra=[t3])
            G["bank", 1] = e3
            G["decrow"] = t3
            wb2 = [G["bank", 2]] if ("bank", 2) in G else []
            t4 = pe(lambda e: e.matmul(pb[2][:, 0:NH], ones1[:],
                                       mrow[0:1, :], start=True, stop=True),
                    extra=[e2b] + wb2, key=("bank", 2))
            G["gtmpN"] = t4
            G["gtmp"] = t2
            v10 = ve(lambda e, par=par: e.tensor_tensor(
                gqT[par][:, 0:NH], pb[2][:, 0:NH], gqT[par][:, 0:NH],
                ALU.subtract), extra=[t4, e1])
            G["bank", 2] = v10
            sch.drain("vector")
            a6 = ae(lambda e, par=par: e.activation(
                gqT[par][:, 0:NH], gqT[par][:, 0:NH], AF.Exp), extra=[v10])
            sch.drain("scalar")
            a7 = ae(lambda e, par=par: e.activation(
                gqT[par][:, 0:NH], gqT[par][:, 0:NH], AF.Copy,
                scale=float(QSC)))
            G["gqT", par] = a7
            gate_done = [a6, a7, e3]
            return dict(par=par, c0=c0, e1=e1, e3=e3, a7=a7, wch=wch)

        def emit_heads(c, ctxd, mid_cb=None):
            par = ctxd["par"]
            e1, e3, a7, wch = ctxd["e1"], ctxd["e3"], ctxd["a7"], ctxd["wch"]

            def front(h):
                hp = h % 2
                tBA = pe(lambda e, h=h, par=par: e.matmul(
                    pb[2 + (h % 2) * 3][:, 0:128],
                    oneh[:, h * 128:(h + 1) * 128], ga[par][:],
                    start=True, stop=True), extra=[e1], key=("bank", 2 + hp * 3))
                wdm0 = [G["dm0", hp]] if ("dm0", hp) in G else []
                aD = ae(lambda e, h=h, par=par, hp=hp: e.activation(
                    dm0[hp][:], pb[2 + hp * 3][:, 0:128], AF.Exp, scale=-1.0,
                    bias=gqT[par][:, NH + h:NH + h + 1]), extra=[tBA, a7] + wdm0)
                G["bank", 2 + hp * 3] = aD
                tG = pe(lambda e, h=h, par=par, hp=hp: e.matmul(
                    pb[3 + hp * 3][:, 0:128], ks[par][:, h, :], qs[par][:, h, :],
                    start=True, stop=True), extra=wch + [e1],
                    key=("bank", 3 + hp * 3))
                wdm = [G["dms", hp]] if ("dms", hp) in G else []
                gD = ge(lambda e, hp=hp: e.tensor_tensor(
                    dms[hp][:], dm0[hp][:], tri[:], ALU.mult), extra=[aD] + wdm)
                G["dm0", hp] = gD
                return dict(aD=aD, tG=tG, gD=gD)

            def back(h, fr):
                hp = h % 2
                aD, tG, gD = fr["aD"], fr["tG"], fr["gD"]
                wss = [G["ssb", hp]] if ("ssb", hp) in G else []
                vS = ve(lambda e, hp=hp: e.tensor_tensor(
                    ssb[hp][:], dms[hp][:], pb[3 + hp * 3][:, 0:128], ALU.mult),
                    extra=[gD, tG] + wss)
                G["dms", hp] = vS
                G["bank", 3 + hp * 3] = vS
                tH = pe(lambda e, h=h, par=par, hp=hp: e.matmul(
                    pb[4 + hp * 3][:, 0:DV + 1], ssb[hp][:], vb[par][:, h, :],
                    start=True, stop=True), extra=[vS], key=("bank", 4 + hp * 3))
                G["ssb", hp] = tH
                tI = pe(lambda e, h=h, par=par, hp=hp: e.matmul(
                    pb[2 + hp * 3][:, 0:DV + 1], qs[par][:, h, :],
                    st[par][:, h, :], start=True, stop=True),
                    extra=[aD] + ([G["stw", par, h]] if ("stw", par, h) in G
                                  else []),
                    key=("bank", 2 + hp * 3))
                G["str", par, h] = tI
                wht = [G["hts", hp]] if ("hts", hp) in G else []
                aH = ae(lambda e, h=h, par=par, hp=hp: e.activation(
                    hts[hp][:], pb[2 + hp * 3][:, 0:DV + 1], AF.Copy,
                    scale=gqT[par][:, h:h + 1]), extra=[tI, a7] + wht)
                G["bank", 2 + hp * 3] = aH
                whn = [G["hn", par, h]] if ("hn", par, h) in G else []
                vH = ve(lambda e, h=h, hp=hp, par=par: e.tensor_tensor(
                    hn[par][:, h, :], hts[hp][:], pb[4 + hp * 3][:, 0:DV + 1],
                    ALU.add), extra=[aH, tH] + whn)
                G["hts", hp] = vH
                G["bank", 4 + hp * 3] = vH
                G["hn", par, h] = vH
                tK = pe(lambda e, h=h, par=par, hp=hp: e.transpose(
                    pb[3 + hp * 3][:, 0:128], ks[par][:, h, :], iden32[:]),
                    extra=[vS], key=("bank", 3 + hp * 3))
                wkt = [G["kts", hp]] if ("kts", hp) in G else []
                aK = ae(lambda e, h=h, par=par, hp=hp: e.activation(
                    kts[hp][:], pb[3 + hp * 3][:, 0:128], AF.Copy,
                    scale=gqT[par][:, 3 * NH + h:3 * NH + h + 1]),
                    extra=[tK, a7] + wkt)
                G["bank", 3 + hp * 3] = aK
                tU = pe(lambda e, h=h, par=par, hp=hp: e.matmul(
                    pb[4 + hp * 3][:, 0:DV + 1], kts[hp][:], vb[par][:, h, :],
                    start=True, stop=True), extra=[aK], key=("bank", 4 + hp * 3))
                G["kts", hp] = tU
                wstw = [G["str", 1 - par, h]] if ("str", 1 - par, h) in G else []
                vU = ve(lambda e, h=h, par=par, hp=hp: e.scalar_tensor_tensor(
                    st[1 - par][:, h, :], st[par][:, h, :],
                    decbc[par][:, h:h + 1], pb[4 + hp * 3][:, 0:DV + 1],
                    ALU.mult, ALU.add), extra=[tU, e3] + wstw)
                G["bank", 4 + hp * 3] = vU
                G["stw", 1 - par, h] = vU
                if h == NH - 1:
                    G["gq", par] = vU
                    G["decbc", par] = vU
                    G["ldU", par] = vU
                    G["ga", par] = vU

            frs = {}
            for h in range(NH):
                frs[h] = front(h)
                if h > 0:
                    back(h - 1, frs.pop(h - 1))
                if h == 3 and mid_cb is not None:
                    mid_cb()
            back(NH - 1, frs.pop(NH - 1))

        def emit_epilogue(c):
            par = c % 2
            c0 = c * 128
            aQ = ae(lambda e, par=par: e.activation(
                qn8[:], hn[par][:, :, DV:DV + 1], AF.Abs),
                extra=[G["hn", par, hx] for hx in range(NH)])
            vD1 = ve(lambda e, par=par: e.tensor_tensor(
                den8[:], qn8[:], gqT[par][:, 2 * NH:3 * NH], ALU.max), extra=[aQ])
            G["gqT", par] = vD1
            sch.drain("vector")
            vD2 = ve(lambda e: e.tensor_scalar_add(den8[:], den8[:], float(MHEPS)))
            sch.drain("vector")
            vD3 = ve(lambda e: e.reciprocal(rec8[:], den8[:]))
            sch.drain("vector")
            vh = ve(lambda e, par=par: e.tensor_tensor(
                hsb.ap().rearrange("p (h v) -> p h v", h=NH),
                hn[par][:, :, 0:DV],
                rec8.ap().rearrange("p (h o) -> p h o", o=1)
                .broadcast_to([128, NH, DV]),
                ALU.mult),
                extra=[vD3] + ([G["hsb_dbg"]] if "hsb_dbg" in G else []))
            for hx in range(NH):
                G["hn", par, hx] = vh
            sch.drain("vector")
            vm1 = ve(lambda e: e.tensor_reduce(
                mean8[:], hsb.ap().rearrange("p (h v) -> p h v", h=NH),
                mybir.AxisListType.X, ALU.add))
            sch.drain("vector")
            vm2 = ve(lambda e: e.tensor_scalar_mul(mean8[:], mean8[:], 1.0 / DV))
            sch.drain("vector")
            vm3 = ve(lambda e: e.tensor_tensor(
                hsb.ap().rearrange("p (h v) -> p h v", h=NH),
                hsb.ap().rearrange("p (h v) -> p h v", h=NH),
                mean8.ap().rearrange("p (h o) -> p h o", o=1)
                .broadcast_to([128, NH, DV]),
                ALU.subtract))
            aS = ae(lambda e: e.activation(sq[:], hsb[:], AF.Square), extra=[vm3])
            vm4 = ve(lambda e: e.tensor_reduce(
                ssum8[:], sq.ap().rearrange("p (h v) -> p h v", h=NH),
                mybir.AxisListType.X, ALU.add), extra=[aS])
            aS2 = ae(lambda e: e.activation(rstd8[:], ssum8[:], AF.Sqrt,
                                            bias=eps_mh[:, 0:1], scale=1.0 / DV),
                     extra=[vm4])
            vm5 = ve(lambda e: e.reciprocal(rstd8[:], rstd8[:]), extra=[aS2])
            sch.drain("vector")
            vm6 = ve(lambda e: e.tensor_tensor(
                hsb.ap().rearrange("p (h v) -> p h v", h=NH),
                hsb.ap().rearrange("p (h v) -> p h v", h=NH),
                rstd8.ap().rearrange("p (h o) -> p h o", o=1)
                .broadcast_to([128, NH, DV]),
                ALU.mult))
            sch.drain("vector")
            gG = ge(lambda e: e.tensor_tensor(hsb[:], hsb[:], gamma_bc[:],
                                              ALU.mult), extra=[vm6, (s_gbe, 4)])
            aO = ae(lambda e, par=par: e.activation(ogs[par][:], ogs[par][:],
                                                    AF.Sigmoid),
                    extra=[(s_ld[par], 64 * (c // 2 + 1))])
            vO = ve(lambda e, par=par: e.tensor_tensor(
                hsb[:], hsb[:], ogs[par][:], ALU.mult), extra=[gG, aO])
            G["ld", par] = vO
            sch.drain("vector")
            if "p2" in debug_taps:
                sch.emit("gpsimd", lambda e, c0=c0: e.dma_start(
                    out=dbg["hout"][c0:c0 + 128, :], in_=hsb[:]),
                    waits=[vO], incs=[(S("dbg2_st", dma="sw"), 16)])
                G["hsb_dbg"] = (S("dbg2_st", dma="sw"),
                                sch.cnt(S("dbg2_st", dma="sw")))
            whoT = [G["houtT"]] if "houtT" in G else []
            evs = []
            for fb in range(KB):
                bk = fb % 2
                tT = pe(lambda e, fb=fb, bk=bk: e.transpose(
                    pb[bk][:, 0:128], hsb[:, fb * 128:(fb + 1) * 128], iden32[:]),
                    extra=[vO] + whoT, key=("bank", bk))
                if fb % 2 == 0:
                    eT = ve(lambda e, fb=fb, bk=bk: e.tensor_copy(
                        houtT[:, fb, :], pb[bk][:, 0:128]), extra=[tT] + whoT)
                else:
                    eT = ae(lambda e, fb=fb, bk=bk: e.activation(
                        houtT[:, fb, :], pb[bk][:, 0:128], AF.Copy),
                        extra=[tT] + whoT)
                G["bank", bk] = eT
                evs.append(eT)
            aHH = ae(lambda e: e.activation(hh[:], houtT[:], AF.Copy),
                     extra=evs + ([G["hh"]] if "hh" in G else []))
            vLL = ve(lambda e: e.tensor_tensor(hl[:], houtT[:], hh[:],
                                               ALU.subtract),
                     extra=[aHH] + ([G["hl"]] if "hl" in G else []))
            G["houtT"] = vLL
            wst = [(s_stst[par], 32 * (c // 2))] if c >= 2 else []
            sch.emit("gpsimd", lambda e, c0=c0: e.dma_start(
                out=hoh_d[:, :, c0:c0 + 128].rearrange("n p t -> p n t"),
                in_=hh[:]),
                waits=[aHH, vLL] + wst, incs=[(s_stst[par], 16)])
            sch.emit("gpsimd", lambda e, c0=c0: e.dma_start(
                out=hol_d[:, :, c0:c0 + 128].rearrange("n p t -> p n t"),
                in_=hl[:]),
                waits=[vLL], incs=[(s_stst[par], 16)])
            G["hh"] = (s_stst[par], sch.cnt(s_stst[par]))
            G["hl"] = G["hh"]

        pro = {0: emit_prologue(0)}
        for c in range(NCH):
            def mid_cb(c=c):
                if c + 1 < NCH:
                    pro[c + 1] = emit_prologue(c + 1)
            emit_heads(c, pro.pop(c), mid_cb=mid_cb)
            emit_epilogue(c)
        sch.barrier(S("bar"), extra_waits={
            "sync": [(s_stst[0], sch.cnt(s_stst[0])),
                     (s_stst[1], sch.cnt(s_stst[1]))]})
        new_scope()
        pctx.close()

    if "p2" in phases:
        phase2()
    midctx.close()

    if "p3" in phases:
        phase1_tm("p3", woutT_d, ya_d, src_h_d=hoh_d, src_l_d=hol_d)

    qwgR_d = scr("qwgR_s", [I, D], BF16)
    qwuR_d = scr("qwuR_s", [I, D], BF16)
    qwdR_d = scr("qwdR_s", [D, I], BF16)
    bg_d = scr("bg_s", [128, I])
    bu_d = scr("bu_s", [128, I])
    bd_d = scr("bd_s", [128, D])

    def phase_qw_all():
        pctx = ExitStack()
        sb = lambda name, shape, dtype=F32: pctx.enter_context(
            nc.sbuf_tensor(name, shape, dtype))
        MOF = I
        wf = [sb(f"pq_wf{i}", [128, MOF]) for i in range(2)]
        ws = [sb(f"pq_ws{i}", [128, 1]) for i in range(2)]
        wr = [sb(f"pq_wr{i}", [128, 1]) for i in range(2)]
        q8 = [sb(f"pq_q8{i}", [128, MOF], F8) for i in range(2)]
        qb = [sb(f"pq_qb{i}", [128, MOF], BF16) for i in range(2)]
        csrow = sb("pq_cs", [1, 128])
        bblk = sb("pq_bb", [128, 128])
        tstage = [sb(f"pq_ts{i}", [128, IB, 256], BF16) for i in range(2)]
        psf = pctx.enter_context(nc.psum_tensor("pq_psf", [128, 128], F32))
        s_ld = [S(f"pq_ld{i}", dma="hw") for i in range(2)]
        s_v = S("pq_v")
        s_a = S("pq_a")
        s_pe = S("pq_pe")
        s_g = S("pq_g")
        s_st = [S(f"pq_st{i}", dma="sw") for i in range(2)]
        s_bst = S("pq_bst", dma="sw")
        s_tl = [S(f"pq_tl{i}", dma="hw") for i in range(2)]
        s_ts = [S(f"pq_ts{i}", dma="sw") for i in range(2)]
        G = {}

        def ve(f, extra=()):
            sch.emit("vector", f, waits=list(extra), incs=[(s_v, 1)])
            return (s_v, sch.cnt(s_v))

        def ae(f, extra=()):
            sch.emit("scalar", f, waits=list(extra), incs=[(s_a, 1)])
            return (s_a, sch.cnt(s_a))

        def pe(f, extra=()):
            sch.emit("tensor", f, waits=list(extra), incs=[(s_pe, 1)])
            return (s_pe, sch.cnt(s_pe))

        def ge(f, extra=()):
            sch.emit("gpsimd", f, waits=list(extra), incs=[(s_g, 1)])
            return (s_g, sch.cnt(s_g))

        rr = [0]

        def quant_tiles(w_d, NT, OF, qR_d, b_d, bcol0):
            for r in range(NT):
                par = rr[0] % 2
                wld = [G["wf", par]] if ("wf", par) in G else []
                sch.emit("sync", lambda e, r=r, par=par, w_d=w_d, OF=OF:
                         e.dma_start(out=wf[par][:, 0:OF], in_=w_d[r, :, :]),
                         waits=wld, incs=[(s_ld[par], 16)])
                wx = [(s_ld[par], 16 * (rr[0] // 2 + 1))]
                wws = [G["ws", par]] if ("ws", par) in G else []
                v1 = ve(lambda e, par=par, OF=OF: e.tensor_reduce(
                    ws[par][:], wf[par][:, 0:OF], mybir.AxisListType.X,
                    ALU.max, apply_absolute_value=True), extra=wx + wws)
                sch.drain("vector")
                v2 = ve(lambda e, par=par: e.reciprocal(wr[par][:], ws[par][:]),
                        extra=([G["wr", par]] if ("wr", par) in G else []))
                wq8 = [G["q8", par]] if ("q8", par) in G else []
                g1 = ve(lambda e, par=par, OF=OF: e.tensor_scalar(
                    q8[par][:, 0:OF], wf[par][:, 0:OF], wr[par][:, 0:1], 224.0,
                    ALU.mult, ALU.mult), extra=[v2] + wq8)
                G["wf", par] = g1
                G["wr", par] = g1
                wqb = [G["qb", par]] if ("qb", par) in G else []
                a1 = ae(lambda e, par=par, OF=OF: e.activation(
                    qb[par][:, 0:OF], q8[par][:, 0:OF], AF.Copy),
                    extra=[g1] + wqb)
                G["q8", par] = a1
                sch.emit("gpsimd", lambda e, r=r, par=par, qR_d=qR_d, OF=OF:
                         e.dma_start(out=qR_d[r * 128:(r + 1) * 128, :],
                                     in_=qb[par][:, 0:OF]),
                         waits=[a1] + ([G["qb", par]] if ("qb", par) in G
                                       else []),
                         incs=[(s_st[par], 16)])
                G["qb", par] = (s_st[par], sch.cnt(s_st[par]))
                v3 = ve(lambda e, par=par: e.tensor_scalar(
                    ws[par][:], ws[par][:], 1.0 / 224.0, None, ALU.mult),
                    extra=[v2, g1])
                sch.drain("vector")
                G["ws", par] = v3
                wpf = [G["psf"]] if "psf" in G else []
                t1 = pe(lambda e, par=par: e.transpose(
                    psf[0:1, 0:128], ws[par][:], iden32f[:]), extra=[v3] + wpf)
                v4 = ve(lambda e: e.tensor_copy(csrow[:], psf[0:1, 0:128]),
                        extra=[t1] + ([G["csrow"]] if "csrow" in G else []))
                t2 = pe(lambda e: e.matmul(psf[:, 0:128], ones1[:],
                                           csrow[0:1, :], start=True,
                                           stop=True), extra=[v4])
                G["csrow"] = t2
                v5 = ve(lambda e: e.tensor_copy(bblk[:], psf[:, 0:128]),
                        extra=[t2] + ([G["bblk"]] if "bblk" in G else []))
                G["psf"] = v5
                sch.emit("gpsimd", lambda e, r=r, b_d=b_d: e.dma_start(
                    out=b_d[:, r * 128:(r + 1) * 128], in_=bblk[:]),
                    waits=[v5, (s_bst, sch.cnt(s_bst))], incs=[(s_bst, 16)])
                G["bblk"] = (s_bst, sch.cnt(s_bst))
                rr[0] += 1

        def transpose_pass(qR_d, qB_d, NBLK, NSUB, wrows):
            for blk in range(NBLK):
                tp = blk % 2
                for sub in range(NSUB):
                    wtl = ([G["tstage", tp]] if (("tstage", tp) in G
                                                 and sub == 0) else []) + \
                          (wrows if blk < 2 and sub == 0 else [])
                    sch.emit("scalar", lambda e, blk=blk, sub=sub, tp=tp,
                             qR_d=qR_d: e.dma_start(
                                 out=tstage[tp][:, sub, :],
                                 in_=qR_d[blk * 256:(blk + 1) * 256,
                                          sub * 128:(sub + 1) * 128],
                                 transpose=True),
                             waits=wtl, incs=[(s_tl[tp], 16)])
                sch.emit("gpsimd", lambda e, blk=blk, tp=tp, qB_d=qB_d,
                         NSUB=NSUB: e.dma_start(
                             out=qB_d[:, blk], in_=tstage[tp][:, 0:NSUB, :]),
                         waits=[(s_tl[tp], sch.cnt(s_tl[tp]))],
                         incs=[(s_ts[tp], 16)])
                G["tstage", tp] = (s_ts[tp], sch.cnt(s_ts[tp]))

        def rowcnt():
            return [(s_st[0], sch.cnt(s_st[0])), (s_st[1], sch.cnt(s_st[1]))]
        quant_tiles(wg_d, IB, D, qwgR_d, bg_d, 0)
        wg_rows = rowcnt()
        quant_tiles(wu_d, IB, D, qwuR_d, bu_d, 0)
        wu_rows = rowcnt()
        transpose_pass(qwgR_d, qwgB_d, I // 256, KB, wg_rows)
        quant_tiles(wd_d, KB, I, qwdR_d, bd_d, 0)
        wd_rows = rowcnt()
        transpose_pass(qwuR_d, qwuB_d, I // 256, KB, wu_rows)
        transpose_pass(qwdR_d, qwdB_d, D // 256, IB, wd_rows)
        sch.barrier(S("bar"), extra_waits={
            "sync": [(s_ts[0], sch.cnt(s_ts[0])), (s_ts[1], sch.cnt(s_ts[1])),
                     (s_bst, sch.cnt(s_bst)),
                     (s_st[0], sch.cnt(s_st[0])),
                     (s_st[1], sch.cnt(s_st[1]))]})
        new_scope()
        pctx.close()

    iden32f = ctx.enter_context(nc.sbuf_tensor("iden32f_c", [128, 128], F32))
    sch.emit("sync", lambda e: e.dma_start(out=iden32f[:], in_=iden32_d[:, :]),
             incs=[(S("i32f", dma="hw"), 16)])
    sch.emit("tensor", lambda e: None, waits=[(S("i32f", dma="hw"), 16)])

    if "pq" in phases:
        phase_qw_all()

    def phase4():
        pctx = ExitStack()
        sb = lambda name, shape, dtype=F32: pctx.enter_context(
            nc.sbuf_tensor(name, shape, dtype))
        psum = lambda name, shape, dtype=F32: pctx.enter_context(
            nc.psum_tensor(name, shape, dtype))
        ICH = 256
        NIC = I // ICH
        HCH = 256
        NHC = D // HCH
        qxT = [sb(f"p4_qxT{i}", [128, KB, FTILE], BF16) for i in range(2)]
        qwg = [sb(f"p4_qwg{i}", [128, KB, ICH], BF16) for i in range(2)]
        qwu = [sb(f"p4_qwu{i}", [128, KB, ICH], BF16) for i in range(2)]
        bgs = [sb("p4_bgs0", [128, ICH])] * 2
        bus = [sb("p4_bus0", [128, ICH])] * 2
        bds = [sb(f"p4_bds{i}", [128, HCH]) for i in range(2)]
        tstg = [sb(f"p4_tstg{i}", [128, 4, 256]) for i in range(2)]
        tstripe = sb("p4_tstripe", [128, I // 2])
        qtb = sb("p4_qtb", [128, I], BF16)
        t32 = [sb(f"p4_t32{i}", [128, ICH]) for i in range(2)]
        gsb = [sb(f"p4_gs{i}", [128, ICH]) for i in range(2)]
        usb = [sb(f"p4_us{i}", [128, ICH]) for i in range(2)]
        rmx1 = [None, None]
        rmax = [sb(f"p4_rmax{i}", [128, 4, 2]) for i in range(2)]
        rmx1[0] = sb("p4_rmx1a", [128, 1])
        rmx1[1] = sb("p4_rmx1b", [128, 1])
        rst = [sb(f"p4_rst{i}", [128, 4]) for i in range(2)]
        qsc4 = [sb(f"p4_qsc4{i}", [128, 4]) for i in range(2)]
        q8b = sb("p4_q8", [128, I // 2], F8)
        stgq = sb("p4_stgq", [128, IB // 2, 128], BF16)
        qwd = [sb(f"p4_qwd{i}", [128, IB, HCH], BF16) for i in range(2)]
        ytmp = [sb(f"p4_yt{i}", [128, HCH]) for i in range(2)]
        qtTg = pctx.enter_context(nc.sbuf_tensor("p4_qtTg", [128, IB, FTILE],
                                                 BF16))
        psG = [psum(f"p4_psG{i}", [128, 512]) for i in range(2)]
        psU = [psum(f"p4_psU{i}", [128, 512]) for i in range(2)]
        psT = [psum(f"p4_psT{i}", [128, 128], BF16) for i in range(2)]
        psY = [psum(f"p4_psY{i}", [128, HCH]) for i in range(2)]
        s_pe = S("p4_pe")
        s_v = S("p4_v")
        s_a = S("p4_a")
        s_g = S("p4_g")
        s_ld = S("p4_ld", dma="hw")
        s_wld = [S(f"p4_wld{i}", dma="hw") for i in range(2)]
        s_dld = [S(f"p4_dld{i}", dma="hw") for i in range(2)]
        s_qst = [S(f"p4_qst{i}", dma="sw") for i in range(2)]
        s_tst = [S(f"p4_tst{i}", dma="sw") for i in range(2)]
        s_tld = [S(f"p4_tld{i}", dma="hw") for i in range(2)]
        s_gat = S("p4_gat", dma="sw")
        s_yst = [S(f"p4_yst{i}", dma="sw") for i in range(2)]
        s_xcp = S("p4_xcp", dma="sw")
        G = {}

        def pe(f, extra=()):
            sch.emit("tensor", f, waits=list(extra), incs=[(s_pe, 1)])
            return (s_pe, sch.cnt(s_pe))

        def ve(f, extra=()):
            sch.emit("vector", f, waits=list(extra), incs=[(s_v, 1)])
            return (s_v, sch.cnt(s_v))

        def ae(f, extra=()):
            sch.emit("scalar", f, waits=list(extra), incs=[(s_a, 1)])
            return (s_a, sch.cnt(s_a))

        def ge(f, extra=()):
            sch.emit("gpsimd", f, waits=list(extra), incs=[(s_g, 1)])
            return (s_g, sch.cnt(s_g))

        s_bdld = [S(f"p4_bdld{i}", dma="hw") for i in range(2)]
        dnf = sb("p4_dnf", [IB, 128])
        dnfT = sb("p4_dnfT", [128, IB])
        s_idx = S("p4_idx")
        sch.emit("vector", lambda e: e.tensor_copy(dnf[:], dnb[:]),
                 incs=[(s_idx, 1)])
        sch.emit("tensor", lambda e: e.transpose(
            psG[0][:, 0:IB], dnf[:], iden32f[0:IB, 0:IB]),
            waits=[(s_idx, 1)], incs=[(s_idx, 1)])
        sch.emit("vector", lambda e: e.tensor_copy(dnfT[:], psG[0][:, 0:IB]),
                 waits=[(s_idx, 2)])
        sch.drain("vector")
        sch.emit("vector", lambda e: e.tensor_copy(idxc[:], dnfT[:]),
                 incs=[(s_idx, 1)])
        G["psG", 0] = (s_idx, 3)
        G["idx_done"] = (s_idx, 3)
        for f in range(NFT):
            sch.emit("gpsimd", lambda e, f=f: e.dma_start(
                out=y_d[f * FTILE:(f + 1) * FTILE, :],
                in_=x_d[f * FTILE:(f + 1) * FTILE, :]), incs=[(s_xcp, 16)])
        for f in range(NFT):
            sch.emit("gpsimd", lambda e, f=f: e.dma_start(
                out=y_d[f * FTILE:(f + 1) * FTILE, :],
                in_=ya_d[f * FTILE:(f + 1) * FTILE, :], accum_op=ALU.add,
                max_dma_last_dim=2048),
                waits=[(s_xcp, 16 * NFT)], incs=[(s_xcp, 16)])

        def emit_p4a(f, half):
            fp = f % 2
            if half == 0:
                wqx = ([G["qxT", fp]] if ("qxT", fp) in G else [])
                sch.emit("sync", lambda e, f=f, fp=fp: e.dma_start(
                    out=qxT[fp][:], in_=qxT_d[:, :, f * FTILE:(f + 1) * FTILE]
                    .rearrange("n p t -> p n t")), waits=wqx, incs=[(s_ld, 16)])
                tile_nqx[f] = sch.cnt(s_ld)
                vR0 = ve(lambda e, fp=fp: e.memset(rmax[fp][:], 0.0),
                         extra=[G["rmax", fp]] if ("rmax", fp) in G else [])
                G["rmv", fp] = vR0
            nqx = tile_nqx[f]
            for ic in (range(NIC // 2) if half == 0 else
                       range(NIC // 2, NIC)):
                wpar = ic % 2
                wld_g = [G[k] for k in (("qw", 0), ("qw", 1)) if k in G]
                sch.emit("sync", lambda e, f=f, ic=ic, wpar=wpar: e.dma_start(
                    out=qwg[wpar][:], in_=qwgB_d[:, ic]),
                    waits=wld_g, incs=[(s_wld[wpar], 16)])
                sch.emit("sync", lambda e, f=f, ic=ic, wpar=wpar: e.dma_start(
                    out=qwu[wpar][:], in_=qwuB_d[:, ic]),
                    incs=[(s_wld[wpar], 16)])
                sch.emit("sync", lambda e, ic=ic, wpar=wpar: e.dma_start(
                    out=bgs[wpar][:], in_=bg_d[:, ic * ICH:(ic + 1) * ICH]),
                    incs=[(s_wld[wpar], 16)])
                sch.emit("sync", lambda e, ic=ic, wpar=wpar: e.dma_start(
                    out=bus[wpar][:], in_=bu_d[:, ic * ICH:(ic + 1) * ICH]),
                    incs=[(s_wld[wpar], 16)])
                nw = 64 * ((f * NIC + ic) // 2 + 1)
                wW = [(s_wld[wpar], nw), (s_ld, nqx)]
                for tc in range(4):
                    if P4LVL < 2:
                        continue
                    gpb = tc % 2
                    wg_ = [G["psG", gpb]] if ("psG", gpb) in G else []
                    lastg = None
                    for hb in range(KB):
                        lastg = pe(lambda e, hb=hb, tc=tc, wpar=wpar, gpb=gpb:
                                   e.matmul(psG[gpb][:, 0:ICH],
                                            qxT[fp][:, hb, tc * 128:(tc + 1) * 128],
                                            qwg[wpar][:, hb, :],
                                            start=(hb == 0), stop=(hb == KB - 1)),
                                   extra=(wW + wg_) if hb == 0 else ())
                    wu_ = [G["psU", gpb]] if ("psU", gpb) in G else []
                    lastu = None
                    for hb in range(KB):
                        lastu = pe(lambda e, hb=hb, tc=tc, wpar=wpar, gpb=gpb:
                                   e.matmul(psU[gpb][:, 0:ICH],
                                            qxT[fp][:, hb, tc * 128:(tc + 1) * 128],
                                            qwu[wpar][:, hb, :],
                                            start=(hb == 0), stop=(hb == KB - 1)),
                                   extra=wu_ if hb == 0 else ())
                    if P4LVL < 3:
                        G["psG", gpb] = lastg
                        G["psU", gpb] = lastu
                        continue
                    bp = tc % 2
                    rcol = rs_x[:, f * 4 + tc:f * 4 + tc + 1]
                    wg1 = [G["gsb", bp]] if ("gsb", bp) in G else []
                    vGn = ve(lambda e, gpb=gpb, rcol=rcol, wpar=wpar, bp=bp:
                             e.scalar_tensor_tensor(
                                 gsb[bp][:], psG[gpb][:, 0:ICH], rcol,
                                 bgs[wpar][:], ALU.mult, ALU.mult),
                             extra=[lastg] + wg1)
                    G["psG", gpb] = vGn
                    aS = ae(lambda e, bp=bp: e.activation(
                        gsb[bp][:], gsb[bp][:], AF.Silu), extra=[vGn])
                    wu1 = [G["usb", bp]] if ("usb", bp) in G else []
                    vU = ve(lambda e, gpb=gpb, rcol=rcol, wpar=wpar, bp=bp:
                            e.scalar_tensor_tensor(
                                usb[bp][:], psU[gpb][:, 0:ICH], rcol,
                                bus[wpar][:], ALU.mult, ALU.mult),
                            extra=[lastu] + wu1)
                    G["psU", gpb] = vU
                    wt = [G["t32", bp]] if ("t32", bp) in G else []
                    vT = ve(lambda e, bp=bp: e.tensor_tensor(
                        t32[bp][:], gsb[bp][:], usb[bp][:], ALU.mult),
                        extra=[aS, vU] + wt)
                    G["usb", bp] = vT
                    G["gsb", bp] = vT
                    wrm = [G["rmx", fp, bp]] if ("rmx", fp, bp) in G else []
                    vM = ve(lambda e, bp=bp: e.tensor_reduce(
                        rmx1[bp][:], t32[bp][:], mybir.AxisListType.X, ALU.max,
                        apply_absolute_value=True), extra=[vT] + wrm)
                    gM2 = ve(lambda e, tc=tc, bp=bp, fp=fp: e.tensor_tensor(
                        rmax[fp][:, tc, bp:bp + 1], rmax[fp][:, tc, bp:bp + 1],
                        rmx1[bp][:], ALU.max), extra=[G["rmv", fp], vM])
                    G["rmx", fp, bp] = gM2
                    vM2 = gM2
                    spar = fp
                    wtb = ([G["tstg", spar]] if (("tstg", spar) in G and tc == 0)
                           else [])
                    gC = ae(lambda e, tc=tc, spar=spar, bp=bp: e.activation(
                        tstg[spar][:, tc, :], t32[bp][:], AF.Copy),
                        extra=[vT] + wtb)
                    G["t32", bp] = gC
                    if tc == 3:
                        G["qw", wpar] = gC
                    if tc == 3 and P4LVL >= 4:
                        sch.emit("gpsimd", lambda e, f=f, ic=ic, spar=spar:
                                 e.dma_start(
                                     out=t_d[f * FTILE:(f + 1) * FTILE,
                                             ic * ICH:(ic + 1) * ICH]
                                     .rearrange("(tc p) c -> p tc c", p=128),
                                     in_=tstg[spar][:]),
                                 waits=[gC], incs=[(s_tst[spar], 16)])
                        G["tstg", spar] = (s_tst[spar],
                                           sch.cnt(s_tst[spar]))
                if P4LVL >= 3:
                    G["rmax_f", fp] = vM2
            if half == 1:
                tile_tst[f] = ((s_tst[0], sch.cnt(s_tst[0])),
                               (s_tst[1], sch.cnt(s_tst[1])))

        def emit_p4b_quant(f):
            fp = f % 2
            if not P4B:
                return
            vC = ve(lambda e, fp=fp: e.tensor_tensor(
                rmax[fp][:, :, 0:1], rmax[fp][:, :, 0:1], rmax[fp][:, :, 1:2],
                ALU.max),
                extra=[G["rmax_f", fp]]
                + [G[k] for k in (("rmx", fp, 0), ("rmx", fp, 1)) if k in G])
            sch.drain("vector")
            vS1 = ve(lambda e, fp=fp: e.tensor_scalar(
                rst[fp][:], rmax[fp][:, :, 0:1], 1.0 / 224.0, None, ALU.mult))
            vS2 = ve(lambda e, fp=fp: e.reciprocal(qsc4[fp][:],
                                                   rmax[fp][:, :, 0:1]))
            sch.drain("vector")
            G["rmax", fp] = vS2
            tile_vs1[f] = vS1
            for tc in range(4):
                for ih in range(2):
                    i0 = ih * (I // 2)
                    wts = ([G["tstripe"]] if "tstripe" in G else [])
                    sch.emit("scalar", lambda e, f=f, tc=tc, i0=i0: e.dma_start(
                        out=tstripe[:],
                        in_=t_d[f * FTILE + tc * 128:f * FTILE + (tc + 1) * 128,
                                i0:i0 + I // 2]),
                        waits=wts + list(tile_tst[f]),
                        incs=[(s_tld[0], 16)])
                    wq8 = [G["q8b"]] if "q8b" in G else []
                    vQ = ve(lambda e, tc=tc: e.tensor_scalar(
                        q8b[:], tstripe[:], qsc4[fp][:, tc:tc + 1], 224.0,
                        ALU.mult, ALU.mult),
                        extra=[vS2, (s_tld[0], sch.cnt(s_tld[0]))] + wq8)
                    G["tstripe"] = vQ
                    wqtb = [G["qtb", ih]] if ("qtb", ih) in G else []
                    aQ = ae(lambda e, i0=i0: e.activation(
                        qtb[:, i0:i0 + I // 2], q8b[:], AF.Copy),
                        extra=[vQ] + wqtb)
                    G["q8b"] = aQ
                    for ibh in range(IB // 2):
                        ib = ih * (IB // 2) + ibh
                        tpb = ib % 2
                        wpb = [G["psT", tpb]] if ("psT", tpb) in G else []
                        tt = pe(lambda e, ib=ib, tpb=tpb: e.transpose(
                            psT[tpb][:], qtb[:, ib * 128:(ib + 1) * 128],
                            iden16[:]), extra=[aQ] + wpb)
                        wstg = [G["stgq"]] if ("stgq" in G and ibh < 2) else []
                        eT = ((ve if ibh % 2 == 0 else ae)(
                            (lambda e, ibh=ibh, tpb=tpb: e.tensor_copy(
                                stgq[:, ibh, :], psT[tpb][:])) if ibh % 2 == 0
                            else (lambda e, ibh=ibh, tpb=tpb: e.activation(
                                stgq[:, ibh, :], psT[tpb][:], AF.Copy)),
                            extra=[tt] + wstg))
                        G["psT", tpb] = eT
                    wev = [(s_v, sch.cnt(s_v)), (s_a, sch.cnt(s_a))]
                    sch.emit("gpsimd", lambda e, tc=tc, ih=ih, i0=i0: e.dma_start(
                        out=qtT_d[i0:i0 + I // 2, tc * 128:(tc + 1) * 128]
                        .rearrange("(n p) t -> p n t", p=128), in_=stgq[:]),
                        waits=wev, incs=[(s_qst[tc % 2], 16)])
                    G["stgq"] = (s_qst[tc % 2], sch.cnt(s_qst[tc % 2]))
                    G["qtb", ih] = (s_qst[tc % 2], sch.cnt(s_qst[tc % 2]))
            wgat = [(s_qst[0], sch.cnt(s_qst[0])), (s_qst[1], sch.cnt(s_qst[1])),
                    G["idx_done"], G["gat_prev"]] if "gat_prev" in G else [
                    (s_qst[0], sch.cnt(s_qst[0])), (s_qst[1], sch.cnt(s_qst[1])),
                    G["idx_done"]]
            for ib in range(IB):
                sch.emit("gpsimd", lambda e, ib=ib: e.indirect_dma_start(
                    out=qtTg.ap()[:, ib, :], out_offset=None,
                    in_=qtT_d[:, :],
                    in_offset=IndirectOffsetOnAxis(idxc[:, ib:ib + 1], axis=0)),
                    waits=wgat if ib == 0 else [], incs=[(s_gat, 16)])
            tile_gat[f] = sch.cnt(s_gat)

        def emit_p4b_down(f):
            fp = f % 2
            ngat = tile_gat[f]
            vS1 = tile_vs1[f]
            for hc in range(NHC):
                dpar = hc % 2
                wdld = [G["qwd", dpar]] if ("qwd", dpar) in G else []
                sch.emit("scalar", lambda e, hc=hc, dpar=dpar: e.dma_start(
                    out=qwd[dpar][:], in_=qwdB_d[:, hc]),
                    waits=wdld, incs=[(s_dld[dpar], 16)])
                wbd = [G["bds", dpar]] if ("bds", dpar) in G else []
                sch.emit("scalar", lambda e, hc=hc, dpar=dpar: e.dma_start(
                    out=bds[dpar][:], in_=bd_d[:, hc * HCH:(hc + 1) * HCH]),
                    waits=wbd, incs=[(s_bdld[dpar], 16)])
                nbd = 16 * ((f * NHC + hc) // 2 + 1)
                nd = 16 * ((f * NHC + hc) // 2 + 1)
                for tc in range(4):
                    ypb = (hc * 4 + tc) % 2
                    wy = [G["psY", ypb]] if ("psY", ypb) in G else []
                    lasty = None
                    for ib in range(IB):
                        lasty = pe(lambda e, hc=hc, tc=tc, ib=ib, dpar=dpar,
                                   ypb=ypb: e.matmul(
                                       psY[ypb][:],
                                       qtTg.ap()[:, ib, tc * 128:(tc + 1) * 128],
                                       qwd[dpar][:, ib, :],
                                       start=(ib == 0), stop=(ib == IB - 1)),
                                   extra=([(s_dld[dpar], nd), (s_gat, ngat)]
                                          + wy) if ib == 0 else ())
                    wyt = [G["ytmp", ypb]] if ("ytmp", ypb) in G else []
                    aY = ae(lambda e, ypb=ypb, tc=tc: e.activation(
                        ytmp[ypb][:], psY[ypb][:], AF.Copy,
                        scale=rst[fp][:, tc:tc + 1]), extra=[lasty, vS1] + wyt)
                    G["psY", ypb] = aY
                    vY = ve(lambda e, ypb=ypb, hc=hc, dpar=dpar: e.tensor_tensor(
                        ytmp[ypb][:], ytmp[ypb][:], bds[dpar][:],
                        ALU.mult), extra=[aY, (s_bdld[dpar], nbd)])
                    if tc == 3:
                        G["bds", dpar] = vY
                    sch.emit("gpsimd", lambda e, f=f, hc=hc, tc=tc, ypb=ypb:
                             e.dma_start(
                                 out=y_d[f * FTILE + tc * 128:
                                         f * FTILE + (tc + 1) * 128,
                                         hc * HCH:(hc + 1) * HCH],
                                 in_=ytmp[ypb][:], accum_op=ALU.add),
                             waits=[vY, (s_xcp, 32 * NFT)],
                             incs=[(s_yst[ypb], 16)])
                    G["ytmp", ypb] = (s_yst[ypb], sch.cnt(s_yst[ypb]))
                    if tc == 3:
                        G["qwd", dpar] = (s_yst[ypb], sch.cnt(s_yst[ypb]))
                if hc == NHC - 1:
                    G["qxT", fp] = (s_yst[1], sch.cnt(s_yst[1]))
            G["gat_prev"] = (s_yst[(NHC - 1) % 2],
                             sch.cnt(s_yst[(NHC - 1) % 2]))

        tile_tst = {}
        tile_nqx = {}
        tile_gat = {}
        tile_vs1 = {}
        for f in range(NFT):
            emit_p4a(f, 0)
            if f >= 1:
                emit_p4b_quant(f - 1)
            emit_p4a(f, 1)
            if f >= 1:
                emit_p4b_down(f - 1)
        emit_p4b_quant(NFT - 1)
        emit_p4b_down(NFT - 1)
        sch.barrier(S("bar"), extra_waits={
            "sync": [(s_yst[0], sch.cnt(s_yst[0])), (s_yst[1], sch.cnt(s_yst[1])),
                     (s_xcp, sch.cnt(s_xcp)),
                     (s_tst[0], sch.cnt(s_tst[0])),
                     (s_tst[1], sch.cnt(s_tst[1]))]})
        new_scope()
        pctx.close()

    if "p4" in phases:
        phase4()

    with nc.Block() as block:
        sch.realize(block)
    ctx.close()
    return nc


def _prep_inputs(ins):
    f32 = np.float32
    wq = np.ascontiguousarray(ins["L0_attn_Wq"].T.astype(f32, copy=False)).reshape(KB, 128, QK)
    wk = np.ascontiguousarray(ins["L0_attn_Wk"].T.astype(f32, copy=False)).reshape(KB, 128, QK)
    wv = np.ascontiguousarray(ins["L0_attn_Wv"].T.astype(f32, copy=False)).reshape(KB, 128, D)
    wog = np.ascontiguousarray(ins["L0_attn_Wog"].T.astype(f32, copy=False)).reshape(KB, 128, D)
    wout = np.ascontiguousarray(ins["L0_attn_Wout"].T.astype(f32, copy=False)).reshape(KB, 128, D)
    wig = np.ascontiguousarray(ins["L0_attn_Wig"].T.astype(f32, copy=False)).reshape(KB, 128, NH)
    wfg = np.ascontiguousarray(ins["L0_attn_Wfg"].T.astype(f32, copy=False)).reshape(KB, 128, NH)
    com = {
        "wqT": wq, "wkT": wk, "wvT": wv, "wogT": wog, "woutT": wout,
        "wigT": wig, "wfgT": wfg,
        "big": np.ascontiguousarray(ins["L0_attn_big"].astype(f32, copy=False).reshape(NH, 1)),
        "bfg": np.ascontiguousarray(ins["L0_attn_bfg"].astype(f32, copy=False).reshape(NH, 1)),
        "gammar": np.ascontiguousarray(ins["L0_attn_gamma"].astype(f32, copy=False).reshape(1, D)),
        "wg": np.ascontiguousarray(ins["L0_ffn_w_gate"].astype(f32, copy=False)).reshape(IB, 128, D),
        "wu": np.ascontiguousarray(ins["L0_ffn_w_up"].astype(f32, copy=False)).reshape(IB, 128, D),
        "wd": np.ascontiguousarray(ins["L0_ffn_w_down"].astype(f32, copy=False)).reshape(KB, 128, I),
        "dn": np.ascontiguousarray(ins["L0_ffn_down_reorder_index"].astype(np.int32, copy=False)),
        "tri": np.triu(np.full((128, 128), QSC, f32)),
        "iden32": np.eye(128, dtype=f32),
        "iden16": np.eye(128, dtype=ml_dtypes.bfloat16),
        "oneh": np.repeat(np.eye(NH, dtype=f32), 128, axis=1),
        "ones1": np.ones((1, 128), f32),
    }
    x = np.asarray(ins["L0_attn_x"], f32)
    maps = []
    for b in range(8):
        m = dict(com)
        m["x"] = np.ascontiguousarray(x[b])
        maps.append(m)
    return maps


def kernel(**inputs):
    maps = _prep_inputs(inputs)
    nc = build_program()
    mybir.codegen_inst_isa_subclasses(nc)
    res = run_bass_kernel_spmd(nc, maps, core_ids=list(range(8)))
    return np.stack([r["y"] for r in res.results], axis=0)


# version 17
# speedup vs baseline: 1.0735x; 1.0735x over previous
import numpy as np
import ml_dtypes
import concourse.bass as bass
import concourse.mybir as mybir
from concourse.bass_core_rust import IndirectOffsetOnAxis
from concourse.bass_utils import run_bass_kernel_spmd
from contextlib import ExitStack

F32 = mybir.dt.float32
BF16 = mybir.dt.bfloat16
F8 = mybir.dt.float8e4
I32 = mybir.dt.int32
I16 = mybir.dt.int16
AF = mybir.ActivationFunctionType
ALU = mybir.AluOpType

T = 2048
D = 2048
NH, DQ, DV = 8, 128, 256
QK = 1024
I = 5632
L = 128
NCH = T // L
KB = D // 128
IB = I // 128
LNEPS = 1e-5
MHEPS = 1e-6
CAP = 15.0
QSC = np.float32(DQ) ** np.float32(-0.5)
FTILE = 512
P4B = True
P4LVL = 4
NFT = T // FTILE


class SemRef:

    def __init__(self, h, base):
        self.h = h
        self.base = base


class Sched:

    def __init__(self, nc):
        self.nc = nc
        self.items = []
        self.counts = {}

    def _abs(self, s):
        return self.counts.get(s.h, 0)

    def inc(self, sem, n=1):
        self.counts[sem.h] = self.counts.get(sem.h, 0) + n
        return self.counts[sem.h] - sem.base

    def cnt(self, sem):
        return self.counts.get(sem.h, 0) - sem.base

    def emit(self, eng_name, fbody, waits=(), incs=()):
        for s, n in incs:
            self.inc(s, n)
        wl = [(s.h, s.base + v) for s, v in waits]
        il = [(s.h, n) for s, n in incs]

        def run(eng):
            for s, v in wl:
                eng.wait_ge(s, v)
            inst = fbody(eng)
            if il:
                if inst is None:
                    inst = eng.nop()
                for s, n in il:
                    inst.then_inc(s, n)
        self.items.append((eng_name, run))

    def drain(self, eng_name):
        self.items.append((eng_name, lambda eng: eng.drain()))

    def barrier(self, sem, extra_waits=()):
        ew = dict(extra_waits) if extra_waits else {}
        tgt = self.cnt(sem) + 5
        for name in ("sync", "tensor", "vector", "scalar", "gpsimd"):
            self.emit(name, lambda e: None, waits=ew.get(name, ()), incs=[(sem, 1)])
        for name in ("sync", "tensor", "vector", "scalar", "gpsimd"):
            self.emit(name, lambda e: None, waits=[(sem, tgt)])

    def realize(self, block):
        for name in ("sync", "tensor", "vector", "scalar", "gpsimd"):
            items = [fn for e, fn in self.items if e == name]

            def body(eng, _items=items):
                for fn in _items:
                    fn(eng)
            getattr(block, name)(body)


def ceil_div(a, b):
    return (a + b - 1) // b


def build_program(debug_taps=(), phases=("p0","p1","p2","p3","pq","p4")):
    nc = bass.Bass("TRN2", target_bir_lowering=False)
    ctx = ExitStack()

    dt_in = {}

    def din(name, shape, dtype=F32):
        dt_in[name] = nc.dram_tensor(name, shape, dtype, kind="ExternalInput").ap()
        return dt_in[name]

    x_d = din("x", [T, D])
    wqT_d = din("wqT", [KB, 128, QK])
    wkT_d = din("wkT", [KB, 128, QK])
    wvT_d = din("wvT", [KB, 128, D])
    wogT_d = din("wogT", [KB, 128, D])
    woutT_d = din("woutT", [KB, 128, D])
    wigT_d = din("wigT", [KB, 128, NH])
    wfgT_d = din("wfgT", [KB, 128, NH])
    big_d = din("big", [NH, 1])
    bfg_d = din("bfg", [NH, 1])
    gamma_d = din("gammar", [1, D])
    wg_d = din("wg", [IB, 128, D])
    wu_d = din("wu", [IB, 128, D])
    wd_d = din("wd", [KB, 128, I])
    dn_d = din("dn", [I], I32)
    tri_d = din("tri", [128, 128])
    iden32_d = din("iden32", [128, 128])
    iden16_d = din("iden16", [128, 128], BF16)
    oneh_d = din("oneh", [NH, NH * 128])
    ones1_d = din("ones1", [1, 128])

    y_d = nc.dram_tensor("y", [T, D], F32, kind="ExternalOutput").ap()

    def scr(name, shape, dtype=F32):
        return nc.dram_tensor(name, shape, dtype, kind="Internal").ap()

    xnh_d = scr("xnh_s", [KB, 128, T], BF16)
    xnl_d = scr("xnl_s", [KB, 128, T], BF16)
    qxT_d = scr("qxT_s", [KB, 128, T], BF16)
    qf_d = scr("qf_s", [NH, 128, T])
    kf_d = scr("kf_s", [NH, 128, T])
    v_d = scr("v_s", [T, D])
    og_d = scr("og_s", [T, D])
    hoh_d = scr("hoh_s", [KB, 128, T], BF16)
    hol_d = scr("hol_s", [KB, 128, T], BF16)
    ya_d = scr("ya_s", [T, D])
    qwgT_d = scr("qwgT_s", [D, I], BF16)
    qwuT_d = scr("qwuT_s", [D, I], BF16)
    qwdT_d = scr("qwdT_s", [I, D], BF16)
    qtT_d = scr("qtT_s", [I, FTILE], BF16)
    t_d = scr("t_s", [T, I])
    csg_d = scr("csg_s", [1, I])
    csu_d = scr("csu_s", [1, I])
    csd_d = scr("csd_s", [1, D])

    dbg = {}
    if debug_taps:
        if "p0" in debug_taps:
            dbg["xn_tm"] = scr("dbg_xn", [T, D])
            dbg["mr"] = scr("dbg_mr", [T, 1])
        if "p2" in debug_taps:
            dbg["h_sb"] = scr("dbg_h", [T, D])
            dbg["hout"] = scr("dbg_hout", [T, D])

    sb = lambda name, shape, dtype=F32: ctx.enter_context(
        nc.sbuf_tensor(name, shape, dtype))

    iden16 = sb("iden16_c", [128, 128], BF16)
    ones1 = sb("ones1_c", [1, 128])
    m_in = sb("m_in_t", [NH, 1])
    rs_x = sb("rs_x", [128, NCH])
    idxc = sb("idxc", [128, IB], I32)
    eps_ln = sb("eps_ln", [128, 1])
    eps_mh = sb("eps_mh", [128, 1])

    sch = Sched(nc)

    sem_scope = {}
    sem_free = {False: [], "hw": [], "sw": []}
    sem_kind = {}
    sem_n = [0]

    def S(name, dma=False):
        if name not in sem_scope:
            if sem_free[dma]:
                h = sem_free[dma].pop()
            else:
                h = ctx.enter_context(nc.semaphore(f"sem{sem_n[0]}"))
                sem_n[0] += 1
            sem_scope[name] = SemRef(h, sch.counts.get(h, 0))
            sem_kind[name] = dma
        return sem_scope[name]

    def new_scope():
        for n in list(sem_scope):
            if n != "bar" and not n.startswith("dbg"):
                sem_free[sem_kind[n]].append(sem_scope.pop(n).h)

    def emit_const_loads():
        s_cl = S("const_ld", dma="hw")
        for dst, src in ((iden16, iden16_d), (ones1, ones1_d)):
            sch.emit("sync", lambda e, d=dst, s=src: e.dma_start(out=d[:], in_=s[:, :]),
                     incs=[(s_cl, 16)])
        bigc = sb("bigc", [NH, 1])
        bfgc = sb("bfgc", [NH, 1])
        sch.emit("sync", lambda e: e.dma_start(out=bigc[:], in_=big_d[:, :]),
                 incs=[(s_cl, 16)])
        sch.emit("sync", lambda e: e.dma_start(out=bfgc[:], in_=bfg_d[:, :]),
                 incs=[(s_cl, 16)])
        dnb = sb("dnb", [IB, 128], I32)
        sch.emit("sync", lambda e: e.dma_start(
            out=dnb[:], in_=dn_d.rearrange("(b p) -> b p", p=128)),
            incs=[(s_cl, 16)])
        n_dma = sch.cnt(s_cl)
        sch.emit("vector", lambda e: e.memset(eps_ln[:], float(LNEPS)))
        sch.emit("vector", lambda e: e.memset(eps_mh[:], float(MHEPS)))
        sch.emit("vector", lambda e: e.tensor_scalar_mul(bigc[:], bigc[:], 1.0 / CAP),
                 waits=[(s_cl, n_dma)])
        sch.emit("vector", lambda e: e.tensor_scalar_mul(bfgc[:], bfgc[:], 1.0 / CAP))
        sch.drain("vector")
        return bigc, bfgc, dnb

    bigc, bfgc, dnb = emit_const_loads()
    midctx = ExitStack()
    ip_t = midctx.enter_context(nc.sbuf_tensor("ip_t", [NH, T], F32))
    fp_t = midctx.enter_context(nc.sbuf_tensor("fp_t", [NH, T], F32))

    def phase0():
        pctx = ExitStack()
        sb = lambda name, shape, dtype=F32: pctx.enter_context(
            nc.sbuf_tensor(name, shape, dtype))
        s_x = [S(f"p0_x{i}", dma="hw") for i in range(2)]
        s_ev_v = S("p0_ev_v")
        s_ev_s = S("p0_ev_s")
        s_st = [S(f"p0_st{i}", dma="sw") for i in range(2)]
        s_tile = S("p0_tile")
        s_tp = S("p0_tp")
        s_h = S("p0_h")
        s_n = S("p0_n")
        s_v = S("p0_v")
        xt = [sb(f"p0_xt{i}", [128, D]) for i in range(2)]
        xn = sb("p0_xn", [128, D])
        xh = [sb(f"p0_xh{i}", [128, D], BF16) for i in range(2)]
        xl = [sb(f"p0_xl{i}", [128, D], BF16) for i in range(2)]
        qxb = [sb(f"p0_qxb{i}", [128, D], BF16) for i in range(2)]
        qx8 = sb("p0_qx8", [128, D], F8)
        mean = sb("p0_mean", [128, 1])
        var = sb("p0_var", [128, 1])
        rstd = sb("p0_rstd", [128, 1])
        sqs = sb("p0_sq", [128, D])
        mr = sb("p0_mr", [128, 1])
        qsc = sb("p0_qsc", [128, 1])
        stg = [sb(f"p0_stg{i}", [128, KB, 128], BF16) for i in range(2)]
        ps = [pctx.enter_context(nc.psum_tensor(f"p0_ps{i}", [128, 128], BF16))
              for i in range(4)]

        for it in range(NCH):
            par = it % 2
            w = [(s_tile, it - 1)] if it >= 2 else []
            sch.emit("sync", lambda e, it=it, par=par: e.dma_start(
                out=xt[par][:], in_=x_d[it * 128:(it + 1) * 128, :]),
                waits=w, incs=[(s_x[par], 16)])
            wld = [(s_x[par], 16 * (it // 2 + 1))]
            wtr = [(s_tp, 48 * (it - 1))] if it >= 2 else []
            sch.emit("vector", lambda e, par=par: e.tensor_reduce(
                mean[:], xt[par][:], mybir.AxisListType.X, ALU.add),
                waits=wld + wtr)
            sch.drain("vector")
            sch.emit("vector", lambda e: e.tensor_scalar_mul(mean[:], mean[:], 1.0 / D))
            sch.drain("vector")
            wdbg = ([(S("dbg_st", dma="sw"), sch.cnt(S("dbg_st", dma="sw")))]
                    if "p0" in debug_taps else [])
            sch.emit("vector", lambda e, par=par: e.tensor_scalar(
                xn[:], xt[par][:], mean[:, 0:1], None, ALU.subtract),
                waits=wdbg, incs=[(S("p0_c"), 1)])
            sch.emit("scalar", lambda e: e.activation(
                sqs[:], xn[:], AF.Square, accum_out=var[:]),
                waits=[(S("p0_c"), sch.cnt(S("p0_c")))])
            sch.drain("scalar")
            sch.emit("scalar", lambda e: e.activation(
                rstd[:], var[:], AF.Sqrt, bias=eps_ln[:, 0:1], scale=1.0 / D),
                incs=[(s_v, 1)])
            sch.emit("vector", lambda e: e.reciprocal(rstd[:], rstd[:]),
                     waits=[(s_v, sch.cnt(s_v))])
            sch.drain("vector")
            sch.emit("vector", lambda e: e.tensor_scalar(
                xn[:], xn[:], rstd[:, 0:1], None, ALU.mult), incs=[(s_n, 1)])
            sch.drain("vector")
            if "p0" in debug_taps:
                sch.emit("gpsimd", lambda e, it=it: e.dma_start(
                    out=dbg["xn_tm"][it * 128:(it + 1) * 128, :], in_=xn[:]),
                    waits=[(s_n, sch.cnt(s_n))], incs=[(S("dbg_st", dma="sw"), 16)])
            sch.emit("vector", lambda e: e.tensor_reduce(
                mr[:], xn[:], mybir.AxisListType.X, ALU.max,
                apply_absolute_value=True))
            sch.drain("vector")
            sch.emit("vector", lambda e: e.reciprocal(qsc[:], mr[:]))
            sch.emit("vector", lambda e, it=it: e.tensor_scalar(
                rs_x[:, it:it + 1], mr[:], 1.0 / 224.0, None, ALU.mult))
            sch.drain("vector")
            sch.emit("vector", lambda e: e.tensor_scalar(
                qx8[:], xn[:], qsc[:, 0:1], 224.0, ALU.mult, ALU.mult))
            sch.drain("vector")
            sch.emit("vector", lambda e, par=par: e.tensor_copy(qxb[par][:], qx8[:]))
            if "p0" in debug_taps:
                sch.emit("gpsimd", lambda e, it=it: e.dma_start(
                    out=dbg["mr"][it * 128:(it + 1) * 128, :], in_=mr[:]),
                    incs=[(S("dbg_st", dma="sw"), 16)])
            sch.emit("scalar", lambda e, par=par: e.activation(
                xh[par][:], xn[:], AF.Copy),
                waits=[(s_n, sch.cnt(s_n))], incs=[(s_h, 1)])
            sch.emit("vector", lambda e, par=par: e.tensor_tensor(
                xl[par][:], xn[:], xh[par][:], ALU.subtract),
                waits=[(s_h, sch.cnt(s_h))], incs=[(s_tile, 1)])
            sch.drain("vector")
            nt = sch.cnt(s_tile)
            if it == 0:
                bank_last = {}
            for si, (src_t, dst_d) in enumerate(
                    ((qxb[par], qxT_d), (xh[par], xnh_d), (xl[par], xnl_d))):
                spar = (it * 3 + si) % 2
                nprev = (it * 3 + si) - 2
                wstg = [(s_st[spar], 16 * (nprev // 2 + 1))] if nprev >= 0 else []
                for fb in range(KB):
                    pb = (it * 48 + si * KB + fb) % 4
                    wtp = [bank_last[pb]] if pb in bank_last else []
                    sch.emit("tensor", lambda e, src_t=src_t, fb=fb, pb=pb: e.transpose(
                        ps[pb][:], src_t[:, fb * 128:(fb + 1) * 128], iden16[:]),
                        waits=[(s_tile, nt)] + wtp, incs=[(s_tp, 1)])
                    eng = "vector" if fb % 2 == 0 else "scalar"
                    sev = s_ev_v if fb % 2 == 0 else s_ev_s
                    if fb % 2 == 0:
                        fevac = (lambda e, fb=fb, pb=pb, spar=spar:
                                 e.tensor_copy(stg[spar][:, fb, :], ps[pb][:]))
                    else:
                        fevac = (lambda e, fb=fb, pb=pb, spar=spar:
                                 e.activation(stg[spar][:, fb, :], ps[pb][:], AF.Copy))
                    sch.emit(eng, fevac,
                             waits=[(s_tp, sch.cnt(s_tp))] + wstg,
                             incs=[(sev, 1)])
                    bank_last[pb] = (sev, sch.cnt(sev))
                wev = [(s_ev_v, sch.cnt(s_ev_v)), (s_ev_s, sch.cnt(s_ev_s))]
                sch.emit("gpsimd", lambda e, dst_d=dst_d, it=it, spar=spar: e.dma_start(
                    out=dst_d[:, :, it * 128:(it + 1) * 128].rearrange("n p t -> p n t"),
                    in_=stg[spar][:]),
                    waits=wev, incs=[(s_st[spar], 16)])
        ntot = NCH * 3
        sch.barrier(S("bar"), extra_waits={
            "sync": [(s_st[0], 16 * ((ntot + 1) // 2)),
                     (s_st[1], 16 * (ntot // 2))]})
        new_scope()
        pctx.close()

    if "p0" in phases:
        phase0()

    def load_split_weight(tag, w_d, OF, wh, wl, s_done, pre_wait=()):
        pctx = ExitStack()
        wf = [pctx.enter_context(nc.sbuf_tensor(f"{tag}_wf{i}", [128, OF], F32))
              for i in range(2)]
        s_ld = [S(f"{tag}_wld{i}", dma="hw") for i in range(2)]
        s_sp = S(f"{tag}_wsp")
        for k in range(KB):
            par = k % 2
            w = [(s_sp, k - 1)] if k >= 2 else list(pre_wait)
            sch.emit("sync", lambda e, k=k, par=par: e.dma_start(
                out=wf[par][:], in_=w_d[k, :, :]), waits=w, incs=[(s_ld[par], 16)])
            sch.emit("scalar", lambda e, k=k, par=par: e.activation(
                wh[:, k, :], wf[par][:], AF.Copy),
                waits=[(s_ld[par], 16 * (k // 2 + 1))], incs=[(S(f"{tag}_wh"), 1)])
            sch.emit("vector", lambda e, k=k, par=par: e.tensor_tensor(
                wl[:, k, :], wf[par][:], wh[:, k, :], ALU.subtract),
                waits=[(S(f"{tag}_wh"), sch.cnt(S(f"{tag}_wh")))],
                incs=[(s_sp, 1)])
        sch.emit("tensor", lambda e: None, waits=[(s_sp, KB)], incs=[(s_done, 1)])
        pctx.close()
        return s_sp

    def phase1_qk():
        pctx = ExitStack()
        sb = lambda name, shape, dtype=F32: pctx.enter_context(
            nc.sbuf_tensor(name, shape, dtype))
        wqh = sb("p1_wqh", [128, KB, QK], BF16)
        wql = sb("p1_wql", [128, KB, QK], BF16)
        wkh = sb("p1_wkh", [128, KB, QK], BF16)
        wkl = sb("p1_wkl", [128, KB, QK], BF16)
        wgh = sb("p1_wgh", [128, KB, 2 * NH], BF16)
        wgl = sb("p1_wgl", [128, KB, 2 * NH], BF16)
        s_wrdy = S("p1qk_wrdy")
        spq = load_split_weight("p1wq", wqT_d, QK, wqh, wql, s_wrdy)
        spk = load_split_weight("p1wk", wkT_d, QK, wkh, wkl, s_wrdy,
                                pre_wait=[(spq, KB)])
        s_gld = [S(f"p1_gld{i}", dma="hw") for i in range(2)]
        gfk = [sb(f"p1_gfk{i}", [128, 2 * NH], F32) for i in range(2)]
        for k in range(KB):
            par = k % 2
            w = [(S("p1_gsp"), k - 1)] if k >= 2 else [(spk, KB)]
            sch.emit("sync", lambda e, k=k, par=par: e.dma_start(
                out=gfk[par][:, 0:NH], in_=wigT_d[k, :, :]), waits=w,
                incs=[(s_gld[par], 16)])
            sch.emit("sync", lambda e, k=k, par=par: e.dma_start(
                out=gfk[par][:, NH:2 * NH], in_=wfgT_d[k, :, :]),
                incs=[(s_gld[par], 16)])
            sch.emit("scalar", lambda e, k=k, par=par: e.activation(
                wgh[:, k, :], gfk[par][:], AF.Copy),
                waits=[(s_gld[par], 32 * (k // 2 + 1))], incs=[(S("p1_gh"), 1)])
            sch.emit("vector", lambda e, k=k, par=par: e.tensor_tensor(
                wgl[:, k, :], gfk[par][:], wgh[:, k, :], ALU.subtract),
                waits=[(S("p1_gh"), sch.cnt(S("p1_gh")))], incs=[(S("p1_gsp"), 1)])
        sch.emit("tensor", lambda e: None, waits=[(S("p1_gsp"), KB)],
                 incs=[(s_wrdy, 1)])
        TCH = 256
        NTC = T // TCH
        xh = [sb(f"p1qk_xh{i}", [128, KB, TCH], BF16) for i in range(2)]
        xl = [sb(f"p1qk_xl{i}", [128, KB, TCH], BF16) for i in range(2)]
        s_x = [S(f"p1qk_x{i}", dma="hw") for i in range(2)]
        s_mm = S("p1qk_mm")
        s_ev_v = S("p1qk_ev_v")
        s_ev_s = S("p1qk_ev_s")
        s_st = [S(f"p1qk_st{i}", dma="sw") for i in range(2)]
        qo = [sb(f"p1qk_qo{i}", [128, TCH], F32) for i in range(2)]
        NPB = 6
        psb = [pctx.enter_context(nc.psum_tensor(f"p1qk_ps{i}", [128, TCH], F32))
               for i in range(NPB)]
        ps = [t[:] for t in psb]
        psg = pctx.enter_context(nc.psum_tensor("p1qk_psg", [NH, TCH], F32))
        psg2 = pctx.enter_context(nc.psum_tensor("p1qk_psg2", [NH, TCH], F32))
        bank_last = {}
        ng = 0
        for tc in range(NTC):
            par = tc % 2
            w = ([(s_mm, (tc - 1) * 16), (S("p1_gmm"), 2 * (tc - 1))]
                 if tc >= 2 else [(s_wrdy, 3)])
            sch.emit("sync", lambda e, tc=tc, par=par: e.dma_start(
                out=xh[par][:],
                in_=xnh_d[:, :, tc * TCH:(tc + 1) * TCH].rearrange("n p t -> p n t")),
                waits=w, incs=[(s_x[par], 16)])
            sch.emit("sync", lambda e, tc=tc, par=par: e.dma_start(
                out=xl[par][:],
                in_=xnl_d[:, :, tc * TCH:(tc + 1) * TCH].rearrange("n p t -> p n t")),
                incs=[(s_x[par], 16)])
            wx = [(s_x[par], 32 * (tc // 2 + 1)), (s_wrdy, 3)]
            for ob in range(16):
                wres_h = (wqh if ob < 8 else wkh)
                wres_l = (wql if ob < 8 else wkl)
                qb = ob % 8
                pb = ng % NPB
                wpb = [bank_last[pb]] if pb in bank_last else []
                last = None
                for k in range(KB):
                    for pi, (wv_, xv_) in enumerate(
                            ((wres_h, xh[par]), (wres_l, xh[par]), (wres_h, xl[par]))):
                        def mmf(e, wv_=wv_, xv_=xv_, k=k, qb=qb, pb=pb,
                                first=(k == 0 and pi == 0),
                                stop=(k == KB - 1 and pi == 2), par=par):
                            return e.matmul(
                                ps[pb], wv_[:, k, qb * 128:(qb + 1) * 128],
                                xv_[:, k, :], start=first, stop=stop)
                        if k == 0 and pi == 0:
                            sch.emit("tensor", mmf, waits=wx + wpb,
                                     incs=[(s_mm, 1)] if (k == KB - 1) else [])
                        elif k == KB - 1 and pi == 2:
                            sch.emit("tensor", mmf, incs=[(s_mm, 1)])
                        else:
                            sch.emit("tensor", mmf)
                ng += 1
                eng = "vector" if ob % 2 == 0 else "scalar"
                sev = s_ev_v if ob % 2 == 0 else s_ev_s
                spar = ng % 2
                nst_prev = ng - 3
                wst = [(s_st[spar], 16 * ((ng - 2 - 1) // 2 + 1))] if ng > 2 else []
                if eng == "vector":
                    fev = lambda e, pb=pb, spar=spar: e.tensor_copy(qo[spar][:], ps[pb])
                else:
                    fev = lambda e, pb=pb, spar=spar: e.activation(qo[spar][:], ps[pb], AF.Copy)
                sch.emit(eng, fev, waits=[(s_mm, ng)] + wst, incs=[(sev, 1)])
                bank_last[pb] = (sev, sch.cnt(sev))
                dst = qf_d if ob < 8 else kf_d
                sch.emit("gpsimd", lambda e, dst=dst, qb=qb, tc=tc, spar=spar:
                         e.dma_start(out=dst[qb, :, tc * TCH:(tc + 1) * TCH],
                                     in_=qo[spar][:]),
                         waits=[(sev, sch.cnt(sev))], incs=[(s_st[spar], 16)])
            for gi, pstgt in ((0, psg), (1, psg2)):
                for k in range(KB):
                    for pi, (wv_, xv_) in enumerate(
                            ((wgh, xh[par]), (wgl, xh[par]), (wgh, xl[par]))):
                        def gmf(e, wv_=wv_, xv_=xv_, k=k, par=par, gi=gi,
                                pstgt=pstgt,
                                first=(k == 0 and pi == 0),
                                stop=(k == KB - 1 and pi == 2)):
                            return e.matmul(
                                pstgt[:], wv_[:, k, gi * NH:(gi + 1) * NH],
                                xv_[:, k, :], start=first, stop=stop)
                        if k == 0 and pi == 0:
                            wg_ = wx + ([bank_last["g", gi]]
                                        if ("g", gi) in bank_last else [])
                            sch.emit("tensor", gmf, waits=wg_)
                        elif k == KB - 1 and pi == 2:
                            sch.emit("tensor", gmf, incs=[(S("p1_gmm"), 1)])
                        else:
                            sch.emit("tensor", gmf)
            sch.emit("scalar", lambda e, tc=tc: e.activation(
                ip_t[:, tc * TCH:(tc + 1) * TCH], psg[0:NH, :], AF.Tanh,
                bias=bigc[:, 0:1], scale=1.0 / CAP),
                waits=[(S("p1_gmm"), sch.cnt(S("p1_gmm")))],
                incs=[(S("p1_gev"), 1)])
            bank_last["g", 0] = (S("p1_gev"), sch.cnt(S("p1_gev")))
            sch.emit("scalar", lambda e, tc=tc: e.activation(
                fp_t[:, tc * TCH:(tc + 1) * TCH], psg2[0:NH, :], AF.Tanh,
                bias=bfgc[:, 0:1], scale=1.0 / CAP), incs=[(S("p1_gev"), 1)])
            bank_last["g", 1] = (S("p1_gev"), sch.cnt(S("p1_gev")))
            sch.drain("scalar")
            sch.emit("scalar", lambda e, tc=tc: e.activation(
                ip_t[:, tc * TCH:(tc + 1) * TCH], ip_t[:, tc * TCH:(tc + 1) * TCH],
                AF.Copy, scale=float(CAP)))
            sch.drain("scalar")
            sch.emit("scalar", lambda e, tc=tc: e.activation(
                fp_t[:, tc * TCH:(tc + 1) * TCH], fp_t[:, tc * TCH:(tc + 1) * TCH],
                AF.Copy, scale=float(CAP)), incs=[(S("p1_gfin"), 1)])
        nq = NTC * 16
        sch.barrier(S("bar"), extra_waits={
            "sync": [(s_st[0], 16 * (nq // 2)), (s_st[1], 16 * (nq // 2))],
            "scalar": [(S("p1_gfin"), sch.cnt(S("p1_gfin")))]})
        new_scope()
        pctx.close()

    def phase1_tm(tag, w_d, out_d, src_h_d=None, src_l_d=None):
        pctx = ExitStack()
        sb = lambda name, shape, dtype=F32: pctx.enter_context(
            nc.sbuf_tensor(name, shape, dtype))
        if src_h_d is None:
            src_h_d, src_l_d = xnh_d, xnl_d
        wh_ = sb(f"{tag}_wh", [128, KB, D], BF16)
        wl_ = sb(f"{tag}_wl", [128, KB, D], BF16)
        s_wrdy = S(f"{tag}_wrdy")
        load_split_weight(f"{tag}w", w_d, D, wh_, wl_, s_wrdy)
        xh = [sb(f"{tag}_xh{i}", [128, KB, 128], BF16) for i in range(2)]
        xl = [sb(f"{tag}_xl{i}", [128, KB, 128], BF16) for i in range(2)]
        s_x = [S(f"{tag}_x{i}", dma="hw") for i in range(2)]
        s_mm = S(f"{tag}_mm")
        s_ev_v = S(f"{tag}_ev_v")
        s_ev_s = S(f"{tag}_ev_s")
        s_st = [S(f"{tag}_st{i}", dma="sw") for i in range(2)]
        vt = [sb(f"{tag}_vt{i}", [128, D], F32) for i in range(2)]
        ps = [pctx.enter_context(nc.psum_tensor(f"{tag}_ps{i}", [128, 512], F32))
              for i in range(8)]
        bank_last = {}
        ng = 0
        for tt in range(NCH):
            par = tt % 2
            w = [(s_mm, (tt - 1) * 4)] if tt >= 2 else [(s_wrdy, 1)]
            sch.emit("sync", lambda e, tt=tt, par=par: e.dma_start(
                out=xh[par][:],
                in_=src_h_d[:, :, tt * 128:(tt + 1) * 128].rearrange("n p t -> p n t")),
                waits=w, incs=[(s_x[par], 16)])
            sch.emit("sync", lambda e, tt=tt, par=par: e.dma_start(
                out=xl[par][:],
                in_=src_l_d[:, :, tt * 128:(tt + 1) * 128].rearrange("n p t -> p n t")),
                incs=[(s_x[par], 16)])
            wx = [(s_x[par], 32 * (tt // 2 + 1)), (s_wrdy, 1)]
            for of in range(4):
                pb = ng % 8
                wpb = [bank_last[pb]] if pb in bank_last else []
                for k in range(KB):
                    for pi, (xv_, wv_) in enumerate(
                            ((xh[par], wh_), (xh[par], wl_), (xl[par], wh_))):
                        def mmf(e, xv_=xv_, wv_=wv_, k=k, of=of, pb=pb,
                                first=(k == 0 and pi == 0),
                                stop=(k == KB - 1 and pi == 2)):
                            return e.matmul(
                                ps[pb][:], xv_[:, k, :],
                                wv_[:, k, of * 512:(of + 1) * 512],
                                start=first, stop=stop)
                        if k == 0 and pi == 0:
                            sch.emit("tensor", mmf, waits=wx + wpb)
                        elif k == KB - 1 and pi == 2:
                            sch.emit("tensor", mmf, incs=[(s_mm, 1)])
                        else:
                            sch.emit("tensor", mmf)
                ng += 1
                eng = "vector" if of % 2 == 0 else "scalar"
                sev = s_ev_v if of % 2 == 0 else s_ev_s
                wst = ([(s_st[par], 16 * ((tt - 2) // 2 + 1))]
                       if tt >= 2 and of in (0, 1) else [])
                if eng == "vector":
                    fev = (lambda e, pb=pb, of=of, par=par:
                           e.tensor_copy(vt[par][:, of * 512:(of + 1) * 512], ps[pb][:]))
                else:
                    fev = (lambda e, pb=pb, of=of, par=par:
                           e.activation(vt[par][:, of * 512:(of + 1) * 512], ps[pb][:], AF.Copy))
                sch.emit(eng, fev, waits=[(s_mm, ng)] + wst, incs=[(sev, 1)])
                bank_last[pb] = (sev, sch.cnt(sev))
            wev = [(s_ev_v, sch.cnt(s_ev_v)), (s_ev_s, sch.cnt(s_ev_s))]
            sch.emit("gpsimd", lambda e, tt=tt, par=par: e.dma_start(
                out=out_d[tt * 128:(tt + 1) * 128, :], in_=vt[par][:]),
                waits=wev, incs=[(s_st[par], 16)])
        sch.barrier(S("bar"), extra_waits={
            "sync": [(s_st[0], 16 * (sch.cnt(s_st[0]) // 16)),
                     (s_st[1], 16 * (sch.cnt(s_st[1]) // 16))]})
        new_scope()
        pctx.close()

    if "p1" in phases:
        phase1_qk()
        phase1_tm("p1v", wvT_d, v_d)
        phase1_tm("p1og", wogT_d, og_d)

    def phase2():
        pctx = ExitStack()
        sb = lambda name, shape, dtype=F32: pctx.enter_context(
            nc.sbuf_tensor(name, shape, dtype))
        psum = lambda name, shape: pctx.enter_context(
            nc.psum_tensor(name, shape, F32))
        tri = sb("p2_tri", [128, 128])
        iden32 = sb("p2_iden32", [128, 128])
        oneh = sb("p2_oneh", [NH, NH * 128])
        gamma_bc = sb("p2_gbc", [128, D])
        gr = sb("p2_gr", [1, D])
        s_cl = S("p2_cl", dma="hw")
        for dst, srcd in ((tri, tri_d), (iden32, iden32_d), (oneh, oneh_d)):
            sch.emit("sync", lambda e, d=dst, s=srcd: e.dma_start(
                out=d[:], in_=s[:, :]), incs=[(s_cl, 16)])
        sch.emit("sync", lambda e: e.dma_start(out=gr[:], in_=gamma_d[:, :]),
                 incs=[(s_cl, 16)])
        ncl = sch.cnt(s_cl)
        pbc_ctx = ExitStack()
        pbc = pbc_ctx.enter_context(nc.psum_tensor("p2_pbc", [128, 512], F32))
        s_gbc = S("p2_gbc_mm")
        s_gbe = S("p2_gbc_ev")
        for c4 in range(4):
            wv_ = [(s_cl, ncl)] if c4 == 0 else [(s_gbe, sch.cnt(s_gbe))]
            sch.emit("tensor", lambda e, c4=c4: e.matmul(
                pbc[:], ones1[:], gr[0:1, c4 * 512:(c4 + 1) * 512],
                start=True, stop=True), waits=wv_, incs=[(s_gbc, 1)])
            sch.emit("vector", lambda e, c4=c4: e.tensor_copy(
                gamma_bc[:, c4 * 512:(c4 + 1) * 512], pbc[:]),
                waits=[(s_gbc, sch.cnt(s_gbc))], incs=[(s_gbe, 1)])
        st = [sb(f"p2_st{i}", [128, NH, DV + 1]) for i in range(2)]
        s_init = S("p2_init")
        sch.emit("vector", lambda e: e.memset(st[0][:], 0.0))
        sch.emit("vector", lambda e: e.memset(m_in[:], 0.0), incs=[(s_init, 1)])
        qs = [sb(f"p2_q{i}", [128, NH, 128]) for i in range(2)]
        ks = [sb(f"p2_k{i}", [128, NH, 128]) for i in range(2)]
        vb = [sb(f"p2_v{i}", [128, NH, DV + 1]) for i in range(2)]
        ogs = [sb(f"p2_og{i}", [128, D]) for i in range(2)]
        for i in range(2):
            sch.emit("vector", lambda e, i=i: e.memset(vb[i][:, :, DV:DV + 1], 1.0),
                     incs=[(s_init, 1)])
        ga = [sb(f"p2_ga{i}", [NH, 128]) for i in range(2)]
        gb = [sb(f"p2_gb{i}", [NH, 128]) for i in range(2)]
        gem = [sb(f"p2_gem{i}", [NH, 128]) for i in range(2)]
        gupd = [sb(f"p2_gupd{i}", [NH, 128]) for i in range(2)]
        gps = [sb(f"p2_gps{i}", [NH, 128]) for i in range(2)]
        gqT = [sb(f"p2_gqT{i}", [128, 4 * NH]) for i in range(2)]
        gtmp = sb("p2_gtmp", [NH + 1, 128])
        nal = sb("p2_nal", [NH, 1])
        dec8 = sb("p2_dec8", [NH, 1])
        decrow = sb("p2_decrow", [1, NH])
        mrow = sb("p2_mrow", [1, NH])
        decbc = [sb(f"p2_decbc{i}", [128, NH]) for i in range(2)]
        zer8 = sb("p2_zer8", [NH, 128])
        sch.emit("vector", lambda e: e.memset(zer8[:], 0.0), incs=[(s_init, 1)])
        dm0 = [sb(f"p2_dm0{i}", [128, 128]) for i in range(2)]
        dms = [sb(f"p2_dms{i}", [128, 128]) for i in range(2)]
        ssb = [sb(f"p2_ss{i}", [128, 128]) for i in range(2)]
        hts = [sb(f"p2_ht{i}", [128, DV + 1]) for i in range(2)]
        kts = [sb(f"p2_kt{i}", [128, 128]) for i in range(2)]
        hn = [sb(f"p2_hn{i}", [128, NH, DV + 1]) for i in range(2)]
        qn8 = sb("p2_qn8", [128, NH])
        den8 = sb("p2_den8", [128, NH])
        rec8 = sb("p2_rec8", [128, NH])
        hsb = sb("p2_h", [128, D])
        mean8 = sb("p2_mean8", [128, NH])
        sq = sb("p2_sq", [128, D])
        ssum8 = sb("p2_ssum8", [128, NH])
        rstd8 = sb("p2_rstd8", [128, NH])
        houtT = sb("p2_houtT", [128, KB, 128])
        hh = sb("p2_hh", [128, KB, 128], BF16)
        hl = sb("p2_hl", [128, KB, 128], BF16)
        pbc_ctx.close()
        pb = [psum(f"p2_pb{i}", [128, 512]) for i in range(8)]
        s_ld = [S(f"p2_ld{i}", dma="hw") for i in range(2)]
        s_pe = S("p2_pe")
        s_v = S("p2_v")
        s_a = S("p2_a")
        s_g = S("p2_g")
        s_stst = [S(f"p2_stst{i}", dma="sw") for i in range(2)]
        G = {}
        G["bank", 0] = (s_gbe, 4)

        def pe(f, extra=(), key=None):
            sch.emit("tensor", f, waits=list(extra) + ([G[key]] if key in G else []),
                     incs=[(s_pe, 1)])
            return (s_pe, sch.cnt(s_pe))

        def ve(f, extra=()):
            sch.emit("vector", f, waits=list(extra), incs=[(s_v, 1)])
            return (s_v, sch.cnt(s_v))

        def ae(f, extra=()):
            sch.emit("scalar", f, waits=list(extra), incs=[(s_a, 1)])
            return (s_a, sch.cnt(s_a))

        def ge(f, extra=()):
            sch.emit("gpsimd", f, waits=list(extra), incs=[(s_g, 1)])
            return (s_g, sch.cnt(s_g))

        def emit_prologue(c):
            par = c % 2
            c0 = c * 128
            wld = ([G["ld", par], G["ldU", par]] if ("ld", par) in G
                   else [(s_init, sch.cnt(s_init))])
            sch.emit("sync", lambda e, c0=c0, par=par: e.dma_start(
                out=qs[par][:], in_=qf_d[:, :, c0:c0 + 128].rearrange("h p t -> p h t")),
                waits=wld, incs=[(s_ld[par], 16)])
            sch.emit("sync", lambda e, c0=c0, par=par: e.dma_start(
                out=ks[par][:], in_=kf_d[:, :, c0:c0 + 128].rearrange("h p t -> p h t")),
                incs=[(s_ld[par], 16)])
            sch.emit("sync", lambda e, c0=c0, par=par: e.dma_start(
                out=vb[par][:, :, 0:DV],
                in_=v_d[c0:c0 + 128, :].rearrange("p (h v) -> p h v", h=NH)),
                incs=[(s_ld[par], 16)])
            sch.emit("sync", lambda e, c0=c0, par=par: e.dma_start(
                out=ogs[par][:], in_=og_d[c0:c0 + 128, :]),
                incs=[(s_ld[par], 16)])
            nld = 64 * (c // 2 + 1)
            wch = [(s_ld[par], nld)]
            ipc = ip_t[:, c0:c0 + 128]
            fpc = fp_t[:, c0:c0 + 128]
            wgq = [G["gq", par]] if ("gq", par) in G else []
            wgt = [G["gtmp"]] if "gtmp" in G else []
            a1 = ae(lambda e, fpc=fpc: e.activation(gtmp[0:NH, :], fpc, AF.Exp,
                                                    scale=-1.0), extra=wgt)
            v1 = ve(lambda e: e.tensor_scalar_add(gtmp[0:NH, :], gtmp[0:NH, :], 1.0), extra=[a1])
            a2 = ae(lambda e: e.activation(gtmp[0:NH, :], gtmp[0:NH, :], AF.Ln), extra=[v1])
            v2 = ve(lambda e, par=par: e.tensor_tensor_scan(
                gps[par][:], gtmp[0:NH, :], zer8[:], 0.0,
                ALU.add, ALU.add), extra=[a2] + wgq)
            v3 = ve(lambda e, par=par, ipc=ipc: e.tensor_tensor(
                gb[par][:], ipc, gps[par][:], ALU.add), extra=[v2])
            sch.drain("vector")
            wmin = [G["m_in"]] if "m_in" in G else []
            v4 = ve(lambda e, par=par: e.tensor_tensor_scan(
                ga[par][:], zer8[:], gb[par][:], m_in[:, 0:1],
                ALU.add, ALU.max), extra=wmin)
            sch.drain("vector")
            v5 = ve(lambda e, par=par: e.tensor_scalar(
                nal[:], ga[par][:, 127:128], -1.0, None, ALU.mult),
                extra=[G["nal"]] if "nal" in G else [])
            a3 = ae(lambda e, par=par: e.activation(
                dec8[:], ga[par][:, 127:128], AF.Exp, bias=m_in[:, 0:1],
                scale=-1.0), extra=[v4] + ([G["dec8"]] if "dec8" in G else []))
            v6 = ve(lambda e, par=par: e.tensor_tensor(
                gtmp[0:NH, 0:1], ga[par][:, 127:128],
                gps[par][:, 127:128], ALU.subtract), extra=[v4])
            sch.drain("vector")
            v7 = ve(lambda e, par=par: e.tensor_tensor(
                gem[par][:], gps[par][:], ga[par][:], ALU.subtract))
            a4 = ae(lambda e, par=par: e.activation(
                gem[par][:], gem[par][:], AF.Exp), extra=[v7])
            a5 = ae(lambda e, par=par: e.activation(
                gupd[par][:], gb[par][:], AF.Exp, bias=nal[:, 0:1]), extra=[v5])
            v9 = a3
            v9b = ve(lambda e: e.tensor_copy(gtmp[0:NH, 2:3], m_in[:]))
            sch.drain("vector")
            v8 = ve(lambda e: e.tensor_copy(m_in[:], gtmp[0:NH, 0:1]),
                    extra=[a3, v6, v9b])
            G["m_in"] = v8
            G["nal"] = a5
            wgqT = [G["gqT", par]] if ("gqT", par) in G else []
            e1 = None
            for gi, (gt_, gw_) in enumerate(
                    ((ga[par], v4), (gb[par], v3), (gem[par], a4), (gupd[par], a5))):
                wb0 = [G["bank", 0]] if ("bank", 0) in G else []
                tgi = pe(lambda e, gt_=gt_, gi=gi: e.transpose(
                    pb[0][:, gi * NH:(gi + 1) * NH], gt_[:],
                    iden32[0:NH, 0:NH]), extra=[gw_] + wb0)
                e1 = ve(lambda e, par=par, gi=gi: e.tensor_copy(
                    gqT[par][:, gi * NH:(gi + 1) * NH],
                    pb[0][:, gi * NH:(gi + 1) * NH]), extra=[tgi] + wgqT)
                G["bank", 0] = e1
            wb1 = [G["bank", 1]] if ("bank", 1) in G else []
            t2 = pe(lambda e: e.transpose(
                pb[1][0:1, 0:NH], dec8[:, 0:1], iden32[0:NH, 0:NH]),
                extra=[v9, wmin0] + wb1 if False else [v9] + wb1)
            wdr = [G["decrow"]] if "decrow" in G else []
            e2 = ve(lambda e: e.tensor_copy(decrow[:], pb[1][0:1, 0:NH]),
                    extra=[t2] + wdr)
            G["dec8"] = t2
            t2b = pe(lambda e: e.transpose(
                pb[1][0:1, 0:NH], gtmp[0:NH, 2:3], iden32[0:NH, 0:NH]),
                extra=[v9b, e2])
            wg2 = [G["gtmpN"]] if "gtmpN" in G else []
            e2b = ve(lambda e: e.tensor_copy(mrow[0:1, :], pb[1][0:1, 0:NH]),
                     extra=[t2b] + wg2)
            G["bank", 1] = e2b
            sch.drain("vector")
            wdb = [G["decbc", par]] if ("decbc", par) in G else []
            t3 = pe(lambda e: e.matmul(pb[1][:, 0:NH], ones1[:], decrow[0:1, :],
                                       start=True, stop=True),
                    extra=[e2] + wdb, key=("bank", 1))
            e3 = ve(lambda e, par=par: e.tensor_copy(decbc[par][:], pb[1][:, 0:NH]),
                    extra=[t3])
            G["bank", 1] = e3
            G["decrow"] = t3
            wb2 = [G["bank", 2]] if ("bank", 2) in G else []
            t4 = pe(lambda e: e.matmul(pb[2][:, 0:NH], ones1[:],
                                       mrow[0:1, :], start=True, stop=True),
                    extra=[e2b] + wb2, key=("bank", 2))
            G["gtmpN"] = t4
            G["gtmp"] = t2
            v10 = ve(lambda e, par=par: e.tensor_tensor(
                gqT[par][:, 0:NH], pb[2][:, 0:NH], gqT[par][:, 0:NH],
                ALU.subtract), extra=[t4, e1])
            G["bank", 2] = v10
            sch.drain("vector")
            a6 = ae(lambda e, par=par: e.activation(
                gqT[par][:, 0:NH], gqT[par][:, 0:NH], AF.Exp), extra=[v10])
            sch.drain("scalar")
            a7 = ae(lambda e, par=par: e.activation(
                gqT[par][:, 0:NH], gqT[par][:, 0:NH], AF.Copy,
                scale=float(QSC)))
            G["gqT", par] = a7
            gate_done = [a6, a7, e3]
            return dict(par=par, c0=c0, e1=e1, e3=e3, a7=a7, wch=wch)

        def emit_heads(c, ctxd, mid_cb=None):
            par = ctxd["par"]
            e1, e3, a7, wch = ctxd["e1"], ctxd["e3"], ctxd["a7"], ctxd["wch"]

            def front(h):
                hp = h % 2
                tBA = pe(lambda e, h=h, par=par: e.matmul(
                    pb[2 + (h % 2) * 3][:, 0:128],
                    oneh[:, h * 128:(h + 1) * 128], ga[par][:],
                    start=True, stop=True), extra=[e1], key=("bank", 2 + hp * 3))
                wdm0 = [G["dm0", hp]] if ("dm0", hp) in G else []
                aD = ae(lambda e, h=h, par=par, hp=hp: e.activation(
                    dm0[hp][:], pb[2 + hp * 3][:, 0:128], AF.Exp, scale=-1.0,
                    bias=gqT[par][:, NH + h:NH + h + 1]), extra=[tBA, a7] + wdm0)
                G["bank", 2 + hp * 3] = aD
                tG = pe(lambda e, h=h, par=par, hp=hp: e.matmul(
                    pb[3 + hp * 3][:, 0:128], ks[par][:, h, :], qs[par][:, h, :],
                    start=True, stop=True), extra=wch + [e1],
                    key=("bank", 3 + hp * 3))
                wdm = [G["dms", hp]] if ("dms", hp) in G else []
                gD = ge(lambda e, hp=hp: e.tensor_tensor(
                    dms[hp][:], dm0[hp][:], tri[:], ALU.mult), extra=[aD] + wdm)
                G["dm0", hp] = gD
                return dict(aD=aD, tG=tG, gD=gD)

            def back(h, fr):
                hp = h % 2
                aD, tG, gD = fr["aD"], fr["tG"], fr["gD"]
                wss = [G["ssb", hp]] if ("ssb", hp) in G else []
                vS = ve(lambda e, hp=hp: e.tensor_tensor(
                    ssb[hp][:], dms[hp][:], pb[3 + hp * 3][:, 0:128], ALU.mult),
                    extra=[gD, tG] + wss)
                G["dms", hp] = vS
                G["bank", 3 + hp * 3] = vS
                tH = pe(lambda e, h=h, par=par, hp=hp: e.matmul(
                    pb[4 + hp * 3][:, 0:DV + 1], ssb[hp][:], vb[par][:, h, :],
                    start=True, stop=True), extra=[vS], key=("bank", 4 + hp * 3))
                G["ssb", hp] = tH
                tI = pe(lambda e, h=h, par=par, hp=hp: e.matmul(
                    pb[2 + hp * 3][:, 0:DV + 1], qs[par][:, h, :],
                    st[par][:, h, :], start=True, stop=True),
                    extra=[aD] + ([G["stw", par, h]] if ("stw", par, h) in G
                                  else []),
                    key=("bank", 2 + hp * 3))
                G["str", par, h] = tI
                wht = [G["hts", hp]] if ("hts", hp) in G else []
                aH = ae(lambda e, h=h, par=par, hp=hp: e.activation(
                    hts[hp][:], pb[2 + hp * 3][:, 0:DV + 1], AF.Copy,
                    scale=gqT[par][:, h:h + 1]), extra=[tI, a7] + wht)
                G["bank", 2 + hp * 3] = aH
                whn = [G["hn", par, h]] if ("hn", par, h) in G else []
                vH = ve(lambda e, h=h, hp=hp, par=par: e.tensor_tensor(
                    hn[par][:, h, :], hts[hp][:], pb[4 + hp * 3][:, 0:DV + 1],
                    ALU.add), extra=[aH, tH] + whn)
                G["hts", hp] = vH
                G["bank", 4 + hp * 3] = vH
                G["hn", par, h] = vH
                tK = pe(lambda e, h=h, par=par, hp=hp: e.transpose(
                    pb[3 + hp * 3][:, 0:128], ks[par][:, h, :], iden32[:]),
                    extra=[vS], key=("bank", 3 + hp * 3))
                wkt = [G["kts", hp]] if ("kts", hp) in G else []
                aK = ae(lambda e, h=h, par=par, hp=hp: e.activation(
                    kts[hp][:], pb[3 + hp * 3][:, 0:128], AF.Copy,
                    scale=gqT[par][:, 3 * NH + h:3 * NH + h + 1]),
                    extra=[tK, a7] + wkt)
                G["bank", 3 + hp * 3] = aK
                tU = pe(lambda e, h=h, par=par, hp=hp: e.matmul(
                    pb[4 + hp * 3][:, 0:DV + 1], kts[hp][:], vb[par][:, h, :],
                    start=True, stop=True), extra=[aK], key=("bank", 4 + hp * 3))
                G["kts", hp] = tU
                wstw = [G["str", 1 - par, h]] if ("str", 1 - par, h) in G else []
                vU = ve(lambda e, h=h, par=par, hp=hp: e.scalar_tensor_tensor(
                    st[1 - par][:, h, :], st[par][:, h, :],
                    decbc[par][:, h:h + 1], pb[4 + hp * 3][:, 0:DV + 1],
                    ALU.mult, ALU.add), extra=[tU, e3] + wstw)
                G["bank", 4 + hp * 3] = vU
                G["stw", 1 - par, h] = vU
                if h == NH - 1:
                    G["gq", par] = vU
                    G["decbc", par] = vU
                    G["ldU", par] = vU
                    G["ga", par] = vU

            frs = {}
            for h in range(NH):
                frs[h] = front(h)
                if h > 0:
                    back(h - 1, frs.pop(h - 1))
                if h == 3 and mid_cb is not None:
                    mid_cb()
            back(NH - 1, frs.pop(NH - 1))

        def emit_epilogue(c):
            par = c % 2
            c0 = c * 128
            aQ = ae(lambda e, par=par: e.activation(
                qn8[:], hn[par][:, :, DV:DV + 1], AF.Abs),
                extra=[G["hn", par, hx] for hx in range(NH)])
            vD1 = ve(lambda e, par=par: e.tensor_tensor(
                den8[:], qn8[:], gqT[par][:, 2 * NH:3 * NH], ALU.max), extra=[aQ])
            G["gqT", par] = vD1
            sch.drain("vector")
            vD2 = ve(lambda e: e.tensor_scalar_add(den8[:], den8[:], float(MHEPS)))
            sch.drain("vector")
            vD3 = ve(lambda e: e.reciprocal(rec8[:], den8[:]))
            sch.drain("vector")
            vh = ve(lambda e, par=par: e.tensor_tensor(
                hsb.ap().rearrange("p (h v) -> p h v", h=NH),
                hn[par][:, :, 0:DV],
                rec8.ap().rearrange("p (h o) -> p h o", o=1)
                .broadcast_to([128, NH, DV]),
                ALU.mult),
                extra=[vD3] + ([G["hsb_dbg"]] if "hsb_dbg" in G else []))
            for hx in range(NH):
                G["hn", par, hx] = vh
            sch.drain("vector")
            vm1 = ve(lambda e: e.tensor_reduce(
                mean8[:], hsb.ap().rearrange("p (h v) -> p h v", h=NH),
                mybir.AxisListType.X, ALU.add))
            sch.drain("vector")
            vm2 = ve(lambda e: e.tensor_scalar_mul(mean8[:], mean8[:], 1.0 / DV))
            sch.drain("vector")
            vm3 = ve(lambda e: e.tensor_tensor(
                hsb.ap().rearrange("p (h v) -> p h v", h=NH),
                hsb.ap().rearrange("p (h v) -> p h v", h=NH),
                mean8.ap().rearrange("p (h o) -> p h o", o=1)
                .broadcast_to([128, NH, DV]),
                ALU.subtract))
            aS = ae(lambda e: e.activation(sq[:], hsb[:], AF.Square), extra=[vm3])
            vm4 = ve(lambda e: e.tensor_reduce(
                ssum8[:], sq.ap().rearrange("p (h v) -> p h v", h=NH),
                mybir.AxisListType.X, ALU.add), extra=[aS])
            aS2 = ae(lambda e: e.activation(rstd8[:], ssum8[:], AF.Sqrt,
                                            bias=eps_mh[:, 0:1], scale=1.0 / DV),
                     extra=[vm4])
            vm5 = ve(lambda e: e.reciprocal(rstd8[:], rstd8[:]), extra=[aS2])
            sch.drain("vector")
            vm6 = ve(lambda e: e.tensor_tensor(
                hsb.ap().rearrange("p (h v) -> p h v", h=NH),
                hsb.ap().rearrange("p (h v) -> p h v", h=NH),
                rstd8.ap().rearrange("p (h o) -> p h o", o=1)
                .broadcast_to([128, NH, DV]),
                ALU.mult))
            sch.drain("vector")
            gG = ge(lambda e: e.tensor_tensor(hsb[:], hsb[:], gamma_bc[:],
                                              ALU.mult), extra=[vm6, (s_gbe, 4)])
            aO = ae(lambda e, par=par: e.activation(ogs[par][:], ogs[par][:],
                                                    AF.Sigmoid),
                    extra=[(s_ld[par], 64 * (c // 2 + 1))])
            vO = ve(lambda e, par=par: e.tensor_tensor(
                hsb[:], hsb[:], ogs[par][:], ALU.mult), extra=[gG, aO])
            G["ld", par] = vO
            sch.drain("vector")
            if "p2" in debug_taps:
                sch.emit("gpsimd", lambda e, c0=c0: e.dma_start(
                    out=dbg["hout"][c0:c0 + 128, :], in_=hsb[:]),
                    waits=[vO], incs=[(S("dbg2_st", dma="sw"), 16)])
                G["hsb_dbg"] = (S("dbg2_st", dma="sw"),
                                sch.cnt(S("dbg2_st", dma="sw")))
            whoT = [G["houtT"]] if "houtT" in G else []
            evs = []
            for fb in range(KB):
                bk = fb % 2
                tT = pe(lambda e, fb=fb, bk=bk: e.transpose(
                    pb[bk][:, 0:128], hsb[:, fb * 128:(fb + 1) * 128], iden32[:]),
                    extra=[vO] + whoT, key=("bank", bk))
                if fb % 2 == 0:
                    eT = ve(lambda e, fb=fb, bk=bk: e.tensor_copy(
                        houtT[:, fb, :], pb[bk][:, 0:128]), extra=[tT] + whoT)
                else:
                    eT = ae(lambda e, fb=fb, bk=bk: e.activation(
                        houtT[:, fb, :], pb[bk][:, 0:128], AF.Copy),
                        extra=[tT] + whoT)
                G["bank", bk] = eT
                evs.append(eT)
            aHH = ae(lambda e: e.activation(hh[:], houtT[:], AF.Copy),
                     extra=evs + ([G["hh"]] if "hh" in G else []))
            vLL = ve(lambda e: e.tensor_tensor(hl[:], houtT[:], hh[:],
                                               ALU.subtract),
                     extra=[aHH] + ([G["hl"]] if "hl" in G else []))
            G["houtT"] = vLL
            wst = [(s_stst[par], 32 * (c // 2))] if c >= 2 else []
            sch.emit("gpsimd", lambda e, c0=c0: e.dma_start(
                out=hoh_d[:, :, c0:c0 + 128].rearrange("n p t -> p n t"),
                in_=hh[:]),
                waits=[aHH, vLL] + wst, incs=[(s_stst[par], 16)])
            sch.emit("gpsimd", lambda e, c0=c0: e.dma_start(
                out=hol_d[:, :, c0:c0 + 128].rearrange("n p t -> p n t"),
                in_=hl[:]),
                waits=[vLL], incs=[(s_stst[par], 16)])
            G["hh"] = (s_stst[par], sch.cnt(s_stst[par]))
            G["hl"] = G["hh"]

        pro = {0: emit_prologue(0)}
        for c in range(NCH):
            def mid_cb(c=c):
                if c + 1 < NCH:
                    pro[c + 1] = emit_prologue(c + 1)
            emit_heads(c, pro.pop(c), mid_cb=mid_cb)
            emit_epilogue(c)
        sch.barrier(S("bar"), extra_waits={
            "sync": [(s_stst[0], sch.cnt(s_stst[0])),
                     (s_stst[1], sch.cnt(s_stst[1]))]})
        new_scope()
        pctx.close()

    if "p2" in phases:
        phase2()
    midctx.close()

    if "p3" in phases:
        phase1_tm("p3", woutT_d, ya_d, src_h_d=hoh_d, src_l_d=hol_d)

    qwgR_d = scr("qwgR_s", [I, D], BF16)
    qwuR_d = scr("qwuR_s", [I, D], BF16)
    qwdR_d = scr("qwdR_s", [D, I], BF16)
    bg_d = scr("bg_s", [128, I])
    bu_d = scr("bu_s", [128, I])
    bd_d = scr("bd_s", [128, D])

    def phase_qw_all():
        pctx = ExitStack()
        sb = lambda name, shape, dtype=F32: pctx.enter_context(
            nc.sbuf_tensor(name, shape, dtype))
        MOF = I
        wf = [sb(f"pq_wf{i}", [128, MOF]) for i in range(2)]
        ws = [sb(f"pq_ws{i}", [128, 1]) for i in range(2)]
        wr = [sb(f"pq_wr{i}", [128, 1]) for i in range(2)]
        q8 = [sb(f"pq_q8{i}", [128, MOF], F8) for i in range(2)]
        qb = [sb(f"pq_qb{i}", [128, MOF], BF16) for i in range(2)]
        csrow = sb("pq_cs", [1, 128])
        bblk = sb("pq_bb", [128, 128])
        tstage = [sb(f"pq_ts{i}", [128, I], BF16) for i in range(2)]
        psf = pctx.enter_context(nc.psum_tensor("pq_psf", [128, 128], F32))
        s_ld = [S(f"pq_ld{i}", dma="hw") for i in range(2)]
        s_v = S("pq_v")
        s_a = S("pq_a")
        s_pe = S("pq_pe")
        s_g = S("pq_g")
        s_st = [S(f"pq_st{i}", dma="sw") for i in range(2)]
        s_bst = S("pq_bst", dma="sw")
        s_tl = [S(f"pq_tl{i}", dma="hw") for i in range(2)]
        s_ts = [S(f"pq_ts{i}", dma="sw") for i in range(2)]
        G = {}

        def ve(f, extra=()):
            sch.emit("vector", f, waits=list(extra), incs=[(s_v, 1)])
            return (s_v, sch.cnt(s_v))

        def ae(f, extra=()):
            sch.emit("scalar", f, waits=list(extra), incs=[(s_a, 1)])
            return (s_a, sch.cnt(s_a))

        def pe(f, extra=()):
            sch.emit("tensor", f, waits=list(extra), incs=[(s_pe, 1)])
            return (s_pe, sch.cnt(s_pe))

        def ge(f, extra=()):
            sch.emit("gpsimd", f, waits=list(extra), incs=[(s_g, 1)])
            return (s_g, sch.cnt(s_g))

        rr = [0]

        def quant_tiles(w_d, NT, OF, qR_d, b_d, bcol0):
            for r in range(NT):
                par = rr[0] % 2
                wld = [G["wf", par]] if ("wf", par) in G else []
                sch.emit("sync", lambda e, r=r, par=par, w_d=w_d, OF=OF:
                         e.dma_start(out=wf[par][:, 0:OF], in_=w_d[r, :, :]),
                         waits=wld, incs=[(s_ld[par], 16)])
                wx = [(s_ld[par], 16 * (rr[0] // 2 + 1))]
                wws = [G["ws", par]] if ("ws", par) in G else []
                v1 = ve(lambda e, par=par, OF=OF: e.tensor_reduce(
                    ws[par][:], wf[par][:, 0:OF], mybir.AxisListType.X,
                    ALU.max, apply_absolute_value=True), extra=wx + wws)
                sch.drain("vector")
                v2 = ve(lambda e, par=par: e.reciprocal(wr[par][:], ws[par][:]),
                        extra=([G["wr", par]] if ("wr", par) in G else []))
                wq8 = [G["q8", par]] if ("q8", par) in G else []
                g1 = ve(lambda e, par=par, OF=OF: e.tensor_scalar(
                    q8[par][:, 0:OF], wf[par][:, 0:OF], wr[par][:, 0:1], 224.0,
                    ALU.mult, ALU.mult), extra=[v2] + wq8)
                G["wf", par] = g1
                G["wr", par] = g1
                wqb = [G["qb", par]] if ("qb", par) in G else []
                a1 = ae(lambda e, par=par, OF=OF: e.activation(
                    qb[par][:, 0:OF], q8[par][:, 0:OF], AF.Copy),
                    extra=[g1] + wqb)
                G["q8", par] = a1
                sch.emit("gpsimd", lambda e, r=r, par=par, qR_d=qR_d, OF=OF:
                         e.dma_start(out=qR_d[r * 128:(r + 1) * 128, :],
                                     in_=qb[par][:, 0:OF]),
                         waits=[a1] + ([G["qb", par]] if ("qb", par) in G
                                       else []),
                         incs=[(s_st[par], 16)])
                G["qb", par] = (s_st[par], sch.cnt(s_st[par]))
                v3 = ve(lambda e, par=par: e.tensor_scalar(
                    ws[par][:], ws[par][:], 1.0 / 224.0, None, ALU.mult),
                    extra=[v2, g1])
                sch.drain("vector")
                G["ws", par] = v3
                wpf = [G["psf"]] if "psf" in G else []
                t1 = pe(lambda e, par=par: e.transpose(
                    psf[0:1, 0:128], ws[par][:], iden32f[:]), extra=[v3] + wpf)
                v4 = ve(lambda e: e.tensor_copy(csrow[:], psf[0:1, 0:128]),
                        extra=[t1] + ([G["csrow"]] if "csrow" in G else []))
                t2 = pe(lambda e: e.matmul(psf[:, 0:128], ones1[:],
                                           csrow[0:1, :], start=True,
                                           stop=True), extra=[v4])
                G["csrow"] = t2
                v5 = ve(lambda e: e.tensor_copy(bblk[:], psf[:, 0:128]),
                        extra=[t2] + ([G["bblk"]] if "bblk" in G else []))
                G["psf"] = v5
                sch.emit("gpsimd", lambda e, r=r, b_d=b_d: e.dma_start(
                    out=b_d[:, r * 128:(r + 1) * 128], in_=bblk[:]),
                    waits=[v5, (s_bst, sch.cnt(s_bst))], incs=[(s_bst, 16)])
                G["bblk"] = (s_bst, sch.cnt(s_bst))
                rr[0] += 1

        def transpose_pass(qR_d, qT_d, NT, OF, wrows):
            OB = OF // 128
            for ob in range(OB):
                tp = ob % 2
                wtl = ([G["tstage", tp]] if ("tstage", tp) in G else []) + \
                      (wrows if ob < 2 else [])
                sch.emit("scalar", lambda e, ob=ob, tp=tp, qR_d=qR_d, NT=NT:
                         e.dma_start(out=tstage[tp][:, 0:NT * 128],
                                     in_=qR_d[:, ob * 128:(ob + 1) * 128],
                                     transpose=True),
                         waits=wtl, incs=[(s_tl[tp], 16)])
                sch.emit("gpsimd", lambda e, ob=ob, tp=tp, qT_d=qT_d, NT=NT:
                         e.dma_start(out=qT_d[ob * 128:(ob + 1) * 128, :],
                                     in_=tstage[tp][:, 0:NT * 128]),
                         waits=[(s_tl[tp], sch.cnt(s_tl[tp]))],
                         incs=[(s_ts[tp], 16)])
                G["tstage", tp] = (s_ts[tp], sch.cnt(s_ts[tp]))

        def rowcnt():
            return [(s_st[0], sch.cnt(s_st[0])), (s_st[1], sch.cnt(s_st[1]))]
        quant_tiles(wg_d, IB, D, qwgR_d, bg_d, 0)
        wg_rows = rowcnt()
        quant_tiles(wu_d, IB, D, qwuR_d, bu_d, 0)
        wu_rows = rowcnt()
        transpose_pass(qwgR_d, qwgT_d, IB, D, wg_rows)
        quant_tiles(wd_d, KB, I, qwdR_d, bd_d, 0)
        wd_rows = rowcnt()
        transpose_pass(qwuR_d, qwuT_d, IB, D, wu_rows)
        transpose_pass(qwdR_d, qwdT_d, KB, I, wd_rows)
        sch.barrier(S("bar"), extra_waits={
            "sync": [(s_ts[0], sch.cnt(s_ts[0])), (s_ts[1], sch.cnt(s_ts[1])),
                     (s_bst, sch.cnt(s_bst)),
                     (s_st[0], sch.cnt(s_st[0])),
                     (s_st[1], sch.cnt(s_st[1]))]})
        new_scope()
        pctx.close()

    iden32f = ctx.enter_context(nc.sbuf_tensor("iden32f_c", [128, 128], F32))
    sch.emit("sync", lambda e: e.dma_start(out=iden32f[:], in_=iden32_d[:, :]),
             incs=[(S("i32f", dma="hw"), 16)])
    sch.emit("tensor", lambda e: None, waits=[(S("i32f", dma="hw"), 16)])

    if "pq" in phases:
        phase_qw_all()

    def phase4():
        pctx = ExitStack()
        sb = lambda name, shape, dtype=F32: pctx.enter_context(
            nc.sbuf_tensor(name, shape, dtype))
        psum = lambda name, shape, dtype=F32: pctx.enter_context(
            nc.psum_tensor(name, shape, dtype))
        ICH = 512
        NIC = I // ICH
        HCH = 256
        NHC = D // HCH
        qxT = [sb("p4_qxT0", [128, KB, FTILE], BF16)] * 2
        qwg = [sb(f"p4_qwg{i}", [128, KB, ICH], BF16) for i in range(2)]
        qwu = [sb(f"p4_qwu{i}", [128, KB, ICH], BF16) for i in range(2)]
        bgs = [sb("p4_bgs0", [128, ICH])] * 2
        bus = [sb("p4_bus0", [128, ICH])] * 2
        bds = [sb(f"p4_bds{i}", [128, HCH]) for i in range(2)]
        tstg = [sb(f"p4_tstg{i}", [128, 2, 256]) for i in range(2)]
        tstripe = sb("p4_tstripe", [128, I // 4])
        qtb = sb("p4_qtb", [128, I // 4], BF16)
        t32 = [sb(f"p4_t32{i}", [128, 256]) for i in range(2)]
        gsb = [sb(f"p4_gs{i}", [128, 256]) for i in range(2)]
        usb = [sb(f"p4_us{i}", [128, 256]) for i in range(2)]
        rmx1 = [None, None]
        rmax = [sb(f"p4_rmax{i}", [128, 4, 2]) for i in range(2)]
        rmx1[0] = sb("p4_rmx1a", [128, 1])
        rmx1[1] = sb("p4_rmx1b", [128, 1])
        rst = [sb(f"p4_rst{i}", [128, 4]) for i in range(2)]
        qsc4 = [sb(f"p4_qsc4{i}", [128, 4]) for i in range(2)]
        q8b = sb("p4_q8", [128, I // 4], F8)
        stgq = sb("p4_stgq", [128, IB // 4, 128], BF16)
        qwd = [sb(f"p4_qwd{i}", [128, IB, HCH], BF16) for i in range(2)]
        ytmp = [sb(f"p4_yt{i}", [128, HCH]) for i in range(2)]
        qtTg = pctx.enter_context(nc.sbuf_tensor("p4_qtTg", [128, IB, FTILE],
                                                 BF16))
        psG = [psum(f"p4_psG{i}", [128, 512]) for i in range(2)]
        psU = [psum(f"p4_psU{i}", [128, 512]) for i in range(2)]
        psT = [psum(f"p4_psT{i}", [128, 128], BF16) for i in range(2)]
        psY = [psum(f"p4_psY{i}", [128, HCH]) for i in range(2)]
        s_pe = S("p4_pe")
        s_v = S("p4_v")
        s_a = S("p4_a")
        s_g = S("p4_g")
        s_ld = S("p4_ld", dma="hw")
        s_wld = [S(f"p4_wld{i}", dma="hw") for i in range(2)]
        s_dld = [S(f"p4_dld{i}", dma="hw") for i in range(2)]
        s_qst = [S(f"p4_qst{i}", dma="sw") for i in range(2)]
        s_tst = [S(f"p4_tst{i}", dma="sw") for i in range(2)]
        s_tld = [S(f"p4_tld{i}", dma="hw") for i in range(2)]
        s_gat = S("p4_gat", dma="sw")
        s_yst = [S(f"p4_yst{i}", dma="sw") for i in range(2)]
        s_xcp = S("p4_xcp", dma="sw")
        G = {}

        def pe(f, extra=()):
            sch.emit("tensor", f, waits=list(extra), incs=[(s_pe, 1)])
            return (s_pe, sch.cnt(s_pe))

        def ve(f, extra=()):
            sch.emit("vector", f, waits=list(extra), incs=[(s_v, 1)])
            return (s_v, sch.cnt(s_v))

        def ae(f, extra=()):
            sch.emit("scalar", f, waits=list(extra), incs=[(s_a, 1)])
            return (s_a, sch.cnt(s_a))

        def ge(f, extra=()):
            sch.emit("gpsimd", f, waits=list(extra), incs=[(s_g, 1)])
            return (s_g, sch.cnt(s_g))

        s_bdld = [S(f"p4_bdld{i}", dma="hw") for i in range(2)]
        dnf = sb("p4_dnf", [IB, 128])
        dnfT = sb("p4_dnfT", [128, IB])
        s_idx = S("p4_idx")
        sch.emit("vector", lambda e: e.tensor_copy(dnf[:], dnb[:]),
                 incs=[(s_idx, 1)])
        sch.emit("tensor", lambda e: e.transpose(
            psG[0][:, 0:IB], dnf[:], iden32f[0:IB, 0:IB]),
            waits=[(s_idx, 1)], incs=[(s_idx, 1)])
        sch.emit("vector", lambda e: e.tensor_copy(dnfT[:], psG[0][:, 0:IB]),
                 waits=[(s_idx, 2)])
        sch.drain("vector")
        sch.emit("vector", lambda e: e.tensor_copy(idxc[:], dnfT[:]),
                 incs=[(s_idx, 1)])
        G["psG", 0] = (s_idx, 3)
        G["idx_done"] = (s_idx, 3)
        for f in range(NFT):
            sch.emit("gpsimd", lambda e, f=f: e.dma_start(
                out=y_d[f * FTILE:(f + 1) * FTILE, :],
                in_=x_d[f * FTILE:(f + 1) * FTILE, :]), incs=[(s_xcp, 16)])
        for f in range(NFT):
            sch.emit("gpsimd", lambda e, f=f: e.dma_start(
                out=y_d[f * FTILE:(f + 1) * FTILE, :],
                in_=ya_d[f * FTILE:(f + 1) * FTILE, :], accum_op=ALU.add,
                max_dma_last_dim=2048),
                waits=[(s_xcp, 16 * NFT)], incs=[(s_xcp, 16)])

        def emit_p4a(f, half):
            fp = f % 2
            if half == 0:
                wqx = ([G[k] for k in (("qxT", fp), ("qw", 0), ("qw", 1))
                        if k in G])
                sch.emit("sync", lambda e, f=f, fp=fp: e.dma_start(
                    out=qxT[fp][:], in_=qxT_d[:, :, f * FTILE:(f + 1) * FTILE]
                    .rearrange("n p t -> p n t")), waits=wqx, incs=[(s_ld, 16)])
                tile_nqx[f] = sch.cnt(s_ld)
                vR0 = ve(lambda e, fp=fp: e.memset(rmax[fp][:], 0.0),
                         extra=[G["rmax", fp]] if ("rmax", fp) in G else [])
                G["rmv", fp] = vR0
            nqx = tile_nqx[f]
            for ic in (range(NIC // 2) if half == 0 else
                       range(NIC // 2, NIC)):
                gidx = f * NIC + ic
                wpar = gidx % 2
                wld_g = [G[k] for k in (("qw", 0), ("qw", 1)) if k in G]
                sch.emit("sync", lambda e, f=f, ic=ic, wpar=wpar: e.dma_start(
                    out=qwg[wpar][:],
                    in_=qwgT_d[:, ic * ICH:(ic + 1) * ICH]
                    .rearrange("(n p) c -> p n c", p=128)),
                    waits=wld_g, incs=[(s_wld[wpar], 16)])
                sch.emit("sync", lambda e, f=f, ic=ic, wpar=wpar: e.dma_start(
                    out=qwu[wpar][:],
                    in_=qwuT_d[:, ic * ICH:(ic + 1) * ICH]
                    .rearrange("(n p) c -> p n c", p=128)),
                    incs=[(s_wld[wpar], 16)])
                sch.emit("sync", lambda e, ic=ic, wpar=wpar: e.dma_start(
                    out=bgs[wpar][:], in_=bg_d[:, ic * ICH:(ic + 1) * ICH]),
                    incs=[(s_wld[wpar], 16)])
                sch.emit("sync", lambda e, ic=ic, wpar=wpar: e.dma_start(
                    out=bus[wpar][:], in_=bu_d[:, ic * ICH:(ic + 1) * ICH]),
                    incs=[(s_wld[wpar], 16)])
                nw = 64 * (gidx // 2 + 1)
                wW = [(s_wld[wpar], nw), (s_ld, nqx)]
                for tc in range(4):
                    if P4LVL < 2:
                        continue
                    gpb = tc % 2
                    wg_ = [G["psG", gpb]] if ("psG", gpb) in G else []
                    lastg = None
                    for hb in range(KB):
                        lastg = pe(lambda e, hb=hb, tc=tc, wpar=wpar, gpb=gpb:
                                   e.matmul(psG[gpb][:, 0:ICH],
                                            qxT[fp][:, hb, tc * 128:(tc + 1) * 128],
                                            qwg[wpar][:, hb, :],
                                            start=(hb == 0), stop=(hb == KB - 1)),
                                   extra=(wW + wg_) if hb == 0 else ())
                    wu_ = [G["psU", gpb]] if ("psU", gpb) in G else []
                    lastu = None
                    for hb in range(KB):
                        lastu = pe(lambda e, hb=hb, tc=tc, wpar=wpar, gpb=gpb:
                                   e.matmul(psU[gpb][:, 0:ICH],
                                            qxT[fp][:, hb, tc * 128:(tc + 1) * 128],
                                            qwu[wpar][:, hb, :],
                                            start=(hb == 0), stop=(hb == KB - 1)),
                                   extra=wu_ if hb == 0 else ())
                    if P4LVL < 3:
                        G["psG", gpb] = lastg
                        G["psU", gpb] = lastu
                        continue
                    rcol = rs_x[:, f * 4 + tc:f * 4 + tc + 1]
                    for ih2 in range(2):
                        bp = ih2
                        c0 = ih2 * 256
                        wg1 = [G["gsb", bp]] if ("gsb", bp) in G else []
                        vGn = ve(lambda e, gpb=gpb, rcol=rcol, wpar=wpar,
                                 bp=bp, c0=c0: e.scalar_tensor_tensor(
                                     gsb[bp][:], psG[gpb][:, c0:c0 + 256],
                                     rcol, bgs[wpar][:, c0:c0 + 256],
                                     ALU.mult, ALU.mult),
                                 extra=[lastg] + wg1)
                        G["psG", gpb] = vGn
                        aS = ae(lambda e, bp=bp: e.activation(
                            gsb[bp][:], gsb[bp][:], AF.Silu), extra=[vGn])
                        wu1 = [G["usb", bp]] if ("usb", bp) in G else []
                        vU = ve(lambda e, gpb=gpb, rcol=rcol, wpar=wpar,
                                bp=bp, c0=c0: e.scalar_tensor_tensor(
                                    usb[bp][:], psU[gpb][:, c0:c0 + 256],
                                    rcol, bus[wpar][:, c0:c0 + 256],
                                    ALU.mult, ALU.mult),
                                extra=[lastu] + wu1)
                        G["psU", gpb] = vU
                        wt = [G["t32", bp]] if ("t32", bp) in G else []
                        vT = ve(lambda e, bp=bp: e.tensor_tensor(
                            t32[bp][:], gsb[bp][:], usb[bp][:], ALU.mult),
                            extra=[aS, vU] + wt)
                        G["usb", bp] = vT
                        G["gsb", bp] = vT
                        wrm = ([G["rmx", fp, bp]] if ("rmx", fp, bp) in G
                               else [])
                        vM = ve(lambda e, bp=bp: e.tensor_reduce(
                            rmx1[bp][:], t32[bp][:], mybir.AxisListType.X,
                            ALU.max, apply_absolute_value=True),
                            extra=[vT] + wrm)
                        gM2 = ve(lambda e, tc=tc, bp=bp, fp=fp: e.tensor_tensor(
                            rmax[fp][:, tc, bp:bp + 1],
                            rmax[fp][:, tc, bp:bp + 1],
                            rmx1[bp][:], ALU.max), extra=[G["rmv", fp], vM])
                        G["rmx", fp, bp] = gM2
                        vM2 = gM2
                        spar = ih2
                        wtb = ([G["tstg", spar]] if (("tstg", spar) in G
                                                     and tc % 2 == 0) else [])
                        gC = ae(lambda e, tc=tc, spar=spar, bp=bp: e.activation(
                            tstg[spar][:, tc % 2, :], t32[bp][:], AF.Copy),
                            extra=[vT] + wtb)
                        G["t32", bp] = gC
                        if tc == 3 and ih2 == 1:
                            G["qw", wpar] = gC
                        if tc % 2 == 1 and P4LVL >= 4:
                            icc = ic * 2 + ih2
                            th = tc // 2
                            sch.emit("gpsimd", lambda e, f=f, icc=icc, th=th,
                                     spar=spar: e.dma_start(
                                         out=t_d[f * FTILE + th * 256:
                                                 f * FTILE + (th + 1) * 256,
                                                 icc * 256:(icc + 1) * 256]
                                         .rearrange("(tc p) c -> p tc c",
                                                    p=128),
                                         in_=tstg[spar][:]),
                                     waits=[gC], incs=[(s_tst[spar], 16)])
                            G["tstg", spar] = (s_tst[spar],
                                               sch.cnt(s_tst[spar]))
                if P4LVL >= 3:
                    G["rmax_f", fp] = vM2
            if half == 1:
                tile_tst[f] = ((s_tst[0], sch.cnt(s_tst[0])),
                               (s_tst[1], sch.cnt(s_tst[1])))

        def emit_p4b_quant(f):
            fp = f % 2
            if not P4B:
                return
            vC = ve(lambda e, fp=fp: e.tensor_tensor(
                rmax[fp][:, :, 0:1], rmax[fp][:, :, 0:1], rmax[fp][:, :, 1:2],
                ALU.max),
                extra=[G["rmax_f", fp]]
                + [G[k] for k in (("rmx", fp, 0), ("rmx", fp, 1)) if k in G])
            sch.drain("vector")
            vS1 = ve(lambda e, fp=fp: e.tensor_scalar(
                rst[fp][:], rmax[fp][:, :, 0:1], 1.0 / 224.0, None, ALU.mult))
            vS2 = ve(lambda e, fp=fp: e.reciprocal(qsc4[fp][:],
                                                   rmax[fp][:, :, 0:1]))
            sch.drain("vector")
            G["rmax", fp] = vS2
            tile_vs1[f] = vS1
            for tc in range(4):
                for ih in range(4):
                    i0 = ih * (I // 4)
                    wts = ([G["tstripe"]] if "tstripe" in G else [])
                    sch.emit("scalar", lambda e, f=f, tc=tc, i0=i0: e.dma_start(
                        out=tstripe[:],
                        in_=t_d[f * FTILE + tc * 128:f * FTILE + (tc + 1) * 128,
                                i0:i0 + I // 4]),
                        waits=wts + list(tile_tst[f]),
                        incs=[(s_tld[0], 16)])
                    wq8 = [G["q8b"]] if "q8b" in G else []
                    vQ = ve(lambda e, tc=tc: e.tensor_scalar(
                        q8b[:], tstripe[:], qsc4[fp][:, tc:tc + 1], 224.0,
                        ALU.mult, ALU.mult),
                        extra=[vS2, (s_tld[0], sch.cnt(s_tld[0]))] + wq8)
                    G["tstripe"] = vQ
                    wqtb = [G["qtb"]] if "qtb" in G else []
                    aQ = ae(lambda e: e.activation(
                        qtb[:, 0:I // 4], q8b[:], AF.Copy),
                        extra=[vQ] + wqtb)
                    G["q8b"] = aQ
                    for ibh in range(IB // 4):
                        ib = ih * (IB // 4) + ibh
                        tpb = ib % 2
                        wpb = [G["psT", tpb]] if ("psT", tpb) in G else []
                        tt = pe(lambda e, ibh=ibh, tpb=tpb: e.transpose(
                            psT[tpb][:], qtb[:, ibh * 128:(ibh + 1) * 128],
                            iden16[:]), extra=[aQ] + wpb)
                        wstg = [G["stgq"]] if ("stgq" in G and ibh < 2) else []
                        eT = ((ve if ibh % 2 == 0 else ae)(
                            (lambda e, ibh=ibh, tpb=tpb: e.tensor_copy(
                                stgq[:, ibh, :], psT[tpb][:])) if ibh % 2 == 0
                            else (lambda e, ibh=ibh, tpb=tpb: e.activation(
                                stgq[:, ibh, :], psT[tpb][:], AF.Copy)),
                            extra=[tt] + wstg))
                        G["psT", tpb] = eT
                    wev = [(s_v, sch.cnt(s_v)), (s_a, sch.cnt(s_a))]
                    sch.emit("gpsimd", lambda e, tc=tc, ih=ih, i0=i0: e.dma_start(
                        out=qtT_d[i0:i0 + I // 4, tc * 128:(tc + 1) * 128]
                        .rearrange("(n p) t -> p n t", p=128), in_=stgq[:]),
                        waits=wev, incs=[(s_qst[tc % 2], 16)])
                    G["stgq"] = (s_qst[tc % 2], sch.cnt(s_qst[tc % 2]))
                    G["qtb"] = (s_qst[tc % 2], sch.cnt(s_qst[tc % 2]))
            wgat = [(s_qst[0], sch.cnt(s_qst[0])), (s_qst[1], sch.cnt(s_qst[1])),
                    G["idx_done"], G["gat_prev"]] if "gat_prev" in G else [
                    (s_qst[0], sch.cnt(s_qst[0])), (s_qst[1], sch.cnt(s_qst[1])),
                    G["idx_done"]]
            for ib in range(IB):
                sch.emit("gpsimd", lambda e, ib=ib: e.indirect_dma_start(
                    out=qtTg.ap()[:, ib, :], out_offset=None,
                    in_=qtT_d[:, :],
                    in_offset=IndirectOffsetOnAxis(idxc[:, ib:ib + 1], axis=0)),
                    waits=wgat if ib == 0 else [], incs=[(s_gat, 16)])
            tile_gat[f] = sch.cnt(s_gat)

        def emit_p4b_down(f):
            fp = f % 2
            ngat = tile_gat[f]
            vS1 = tile_vs1[f]
            for hc in range(NHC):
                dpar = hc % 2
                wdld = [G["qwd", dpar]] if ("qwd", dpar) in G else []
                sch.emit("scalar", lambda e, hc=hc, dpar=dpar: e.dma_start(
                    out=qwd[dpar][:],
                    in_=qwdT_d[:, hc * HCH:(hc + 1) * HCH]
                    .rearrange("(n p) c -> p n c", p=128)),
                    waits=wdld, incs=[(s_dld[dpar], 16)])
                wbd = [G["bds", dpar]] if ("bds", dpar) in G else []
                sch.emit("scalar", lambda e, hc=hc, dpar=dpar: e.dma_start(
                    out=bds[dpar][:], in_=bd_d[:, hc * HCH:(hc + 1) * HCH]),
                    waits=wbd, incs=[(s_bdld[dpar], 16)])
                nbd = 16 * ((f * NHC + hc) // 2 + 1)
                nd = 16 * ((f * NHC + hc) // 2 + 1)
                for tc in range(4):
                    ypb = (hc * 4 + tc) % 2
                    wy = [G["psY", ypb]] if ("psY", ypb) in G else []
                    lasty = None
                    for ib in range(IB):
                        lasty = pe(lambda e, hc=hc, tc=tc, ib=ib, dpar=dpar,
                                   ypb=ypb: e.matmul(
                                       psY[ypb][:],
                                       qtTg.ap()[:, ib, tc * 128:(tc + 1) * 128],
                                       qwd[dpar][:, ib, :],
                                       start=(ib == 0), stop=(ib == IB - 1)),
                                   extra=([(s_dld[dpar], nd), (s_gat, ngat)]
                                          + wy) if ib == 0 else ())
                    wyt = [G["ytmp", ypb]] if ("ytmp", ypb) in G else []
                    aY = ae(lambda e, ypb=ypb, tc=tc: e.activation(
                        ytmp[ypb][:], psY[ypb][:], AF.Copy,
                        scale=rst[fp][:, tc:tc + 1]), extra=[lasty, vS1] + wyt)
                    G["psY", ypb] = aY
                    vY = ve(lambda e, ypb=ypb, hc=hc, dpar=dpar: e.tensor_tensor(
                        ytmp[ypb][:], ytmp[ypb][:], bds[dpar][:],
                        ALU.mult), extra=[aY, (s_bdld[dpar], nbd)])
                    if tc == 3:
                        G["bds", dpar] = vY
                    sch.emit("gpsimd", lambda e, f=f, hc=hc, tc=tc, ypb=ypb:
                             e.dma_start(
                                 out=y_d[f * FTILE + tc * 128:
                                         f * FTILE + (tc + 1) * 128,
                                         hc * HCH:(hc + 1) * HCH],
                                 in_=ytmp[ypb][:], accum_op=ALU.add),
                             waits=[vY, (s_xcp, 32 * NFT)],
                             incs=[(s_yst[ypb], 16)])
                    G["ytmp", ypb] = (s_yst[ypb], sch.cnt(s_yst[ypb]))
                    if tc == 3:
                        G["qwd", dpar] = (s_yst[ypb], sch.cnt(s_yst[ypb]))
                if hc == NHC - 1:
                    G["qxT", fp] = (s_yst[1], sch.cnt(s_yst[1]))
            G["gat_prev"] = (s_yst[(NHC - 1) % 2],
                             sch.cnt(s_yst[(NHC - 1) % 2]))

        tile_tst = {}
        tile_nqx = {}
        tile_gat = {}
        tile_vs1 = {}
        for f in range(NFT):
            emit_p4a(f, 0)
            if f >= 1:
                emit_p4b_quant(f - 1)
            emit_p4a(f, 1)
            if f >= 1:
                emit_p4b_down(f - 1)
        emit_p4b_quant(NFT - 1)
        emit_p4b_down(NFT - 1)
        sch.barrier(S("bar"), extra_waits={
            "sync": [(s_yst[0], sch.cnt(s_yst[0])), (s_yst[1], sch.cnt(s_yst[1])),
                     (s_xcp, sch.cnt(s_xcp)),
                     (s_tst[0], sch.cnt(s_tst[0])),
                     (s_tst[1], sch.cnt(s_tst[1]))]})
        new_scope()
        pctx.close()

    if "p4" in phases:
        phase4()

    with nc.Block() as block:
        sch.realize(block)
    ctx.close()
    return nc


def _prep_inputs(ins):
    f32 = np.float32
    wq = np.ascontiguousarray(ins["L0_attn_Wq"].T.astype(f32, copy=False)).reshape(KB, 128, QK)
    wk = np.ascontiguousarray(ins["L0_attn_Wk"].T.astype(f32, copy=False)).reshape(KB, 128, QK)
    wv = np.ascontiguousarray(ins["L0_attn_Wv"].T.astype(f32, copy=False)).reshape(KB, 128, D)
    wog = np.ascontiguousarray(ins["L0_attn_Wog"].T.astype(f32, copy=False)).reshape(KB, 128, D)
    wout = np.ascontiguousarray(ins["L0_attn_Wout"].T.astype(f32, copy=False)).reshape(KB, 128, D)
    wig = np.ascontiguousarray(ins["L0_attn_Wig"].T.astype(f32, copy=False)).reshape(KB, 128, NH)
    wfg = np.ascontiguousarray(ins["L0_attn_Wfg"].T.astype(f32, copy=False)).reshape(KB, 128, NH)
    com = {
        "wqT": wq, "wkT": wk, "wvT": wv, "wogT": wog, "woutT": wout,
        "wigT": wig, "wfgT": wfg,
        "big": np.ascontiguousarray(ins["L0_attn_big"].astype(f32, copy=False).reshape(NH, 1)),
        "bfg": np.ascontiguousarray(ins["L0_attn_bfg"].astype(f32, copy=False).reshape(NH, 1)),
        "gammar": np.ascontiguousarray(ins["L0_attn_gamma"].astype(f32, copy=False).reshape(1, D)),
        "wg": np.ascontiguousarray(ins["L0_ffn_w_gate"].astype(f32, copy=False)).reshape(IB, 128, D),
        "wu": np.ascontiguousarray(ins["L0_ffn_w_up"].astype(f32, copy=False)).reshape(IB, 128, D),
        "wd": np.ascontiguousarray(ins["L0_ffn_w_down"].astype(f32, copy=False)).reshape(KB, 128, I),
        "dn": np.ascontiguousarray(ins["L0_ffn_down_reorder_index"].astype(np.int32, copy=False)),
        "tri": np.triu(np.full((128, 128), QSC, f32)),
        "iden32": np.eye(128, dtype=f32),
        "iden16": np.eye(128, dtype=ml_dtypes.bfloat16),
        "oneh": np.repeat(np.eye(NH, dtype=f32), 128, axis=1),
        "ones1": np.ones((1, 128), f32),
    }
    x = np.asarray(ins["L0_attn_x"], f32)
    maps = []
    for b in range(8):
        m = dict(com)
        m["x"] = np.ascontiguousarray(x[b])
        maps.append(m)
    return maps


def kernel(**inputs):
    maps = _prep_inputs(inputs)
    nc = build_program()
    mybir.codegen_inst_isa_subclasses(nc)
    res = run_bass_kernel_spmd(nc, maps, core_ids=list(range(8)))
    return np.stack([r["y"] for r in res.results], axis=0)
